# version 32
# baseline (speedup 1.0000x reference)
import numpy as np
import concourse.bass as bass
import concourse.mybir as mybir
from concourse.bass_utils import run_bass_kernel_spmd

F32 = mybir.dt.float32
BF16 = mybir.dt.bfloat16
I32 = mybir.dt.int32
AF = mybir.ActivationFunctionType
ALU = mybir.AluOpType
AX = mybir.AxisListType
SP = mybir.EngineType.SP

B, T, C = 4, 4096, 2048
TOWN = 2048
NHA = 8
NKH, NVH, DK, DV = 8, 16, 128, 128
HFF = 5632
KT = C // 128
NCH = T // 128
INV_SQRT_D = 1.0 / np.sqrt(128.0)
LOG_INV_SQRT_D = float(np.log(INV_SQRT_D))

SEMS = ("G", "L", "M", "E", "F", "S", "FS", "FG", "LS", "LG")


class Ser:
    def __init__(self):
        self.steps = []
        self.c = {k: 0 for k in SEMS}
        self.sec_g = None
        self.sec_started = None

    def p(self, eng, fn, waits=(), inc=None):
        waits = list(waits)
        if self.sec_g is not None and eng not in self.sec_started:
            waits = [("G", self.sec_g)] + waits
            self.sec_started.add(eng)
        self.steps.append((eng, fn, [w for w in waits if w[1] > 0], inc))
        if inc is not None:
            self.c[inc[0]] += inc[1]

    def _ser(self, eng, fn, amt):
        assert self.sec_g is None
        self.steps.append((eng, fn, [("G", self.c["G"])] if self.c["G"] else [],
                           ("G", amt)))
        self.c["G"] += amt

    def t(self, fn):
        self._ser("tensor", fn, 1)

    def v(self, fn):
        self._ser("vector", fn, 1)

    def s(self, fn):
        self._ser("scalar", fn, 1)

    def g(self, fn):
        self._ser("gpsimd", fn, 1)

    def d(self, fn):
        self._ser("sync", fn, 16)

    def gd(self, fn):
        self._ser("gpsimd", fn, 16)

    def cc(self, fn):
        self._ser("gpsimd", fn, 1)

    def begin(self):
        assert self.sec_g is None
        self.sec_g = self.c["G"]
        self.sec_started = set()

    def end(self):
        assert self.sec_g is not None
        waits = [(k, self.c[k]) for k in ("L", "M", "E", "F", "S", "FS", "FG", "LS", "LG") if self.c[k] > 0]
        self.steps.append(("sync", lambda e, h: e.nop(), waits, ("G", 1)))
        self.c["G"] += 1
        self.sec_g = None

    def emit_for(self, name, eng, holder, sems):
        for engine, fn, waits, inc in self.steps:
            if engine != name:
                continue
            for nm, val in waits:
                eng.wait_ge(sems[nm], val)
            ins = fn(eng, holder)
            if inc is not None:
                ins.then_inc(sems[inc[0]], inc[1])
        eng.wait_ge(sems["G"], self.c["G"])


def build_nc():
    nc = bass.Bass("TRN2", target_bir_lowering=False)
    dt_in = lambda n, s: nc.dram_tensor(n, s, F32, kind="ExternalInput").ap()

    xT = dt_in("xT", [C, T])
    waT = dt_in("waT", [C, 3072])
    wpT = dt_in("wpT", [C, C])
    wqkvT = dt_in("wqkvT", [C, 4096])
    wzT = dt_in("wzT", [C, 2048])
    wbaT = dt_in("wbaT", [C, 32])
    cwm = dt_in("cwm", [4096, 4])
    dtb = dt_in("dtb", [16, 1])
    alog = dt_in("alog", [16, 1])
    nw = dt_in("nw", [1, 128])
    w1T = dt_in("w1T", [C, HFF])
    w2T = dt_in("w2T", [C, HFF])
    w3T = dt_in("w3T", [HFF, C])
    woT = dt_in("woT", [4096, C])

    yT = nc.dram_tensor("yT", [C, TOWN], F32, kind="ExternalOutput").ap()

    ds = lambda n, s, d=BF16: nc.dram_tensor(n, s, d, kind="Internal").ap()
    s_ln1 = ds("s_ln1", [C, T])
    s_q = ds("s_q", [1024, T])
    s_k = ds("s_k", [1024, T])
    s_vt = ds("s_vt", [T, 1024])
    cc1_in = ds("cc1_in", [1024, T])
    cc1_out = ds("cc1_out", [2048, T])
    s_ao = ds("s_ao", [2048, TOWN])
    s_xo = ds("s_xo", [C, TOWN], F32)
    resid1 = ds("resid1", [C, TOWN], F32)
    resid2 = ds("resid2", [C, TOWN], F32)
    resid3 = ds("resid3", [C, TOWN], F32)
    qh1 = ds("qh1", [HFF // 128, 128 * 1024])
    qh2d = ds("qh2d", [HFF // 128, 128 * 1024])
    qh3 = ds("qh3", [KT, 128 * 2816])
    qg1 = ds("qg1", [2 * (HFF // 128), 128 * 1024])
    qg2 = ds("qg2", [2 * (HFF // 128), 128 * 1024])
    qg3 = ds("qg3", [2 * KT, 128 * 2816])
    s_w1h = ds("s_w1h", [1024, HFF], F32)
    s_w2h = ds("s_w2h", [1024, HFF], F32)
    s_w3h = ds("s_w3h", [2816, C], F32)
    cc4_in = ds("cc4_in", [128, 4], F32)
    cc4_out = ds("cc4_out", [256, 4], F32)
    s_gq3 = ds("s_gq3", [4, 1], F32)
    s_hid = ds("s_hid", [HFF, TOWN])
    s_ln2 = ds("s_ln2", [C, TOWN])
    s_ln4 = ds("s_ln4", [C, TOWN])
    cc2_in = ds("cc2_in", [C, TOWN])
    cc2_out = ds("cc2_out", [2 * C, TOWN])
    s_ln3g = ds("s_ln3g", [C, T])
    s_mx = ds("s_mx", [4096, T])
    s_zt = ds("s_zt", [T, 2048])
    cc3_in = ds("cc3_in", [T, 2048])
    cc3_out = ds("cc3_out", [2 * T, 2048])
    s_cnw = ds("s_cnw", [TOWN, 4096])
    s_cnf = ds("s_cnf", [4096, TOWN])
    s_wo = ds("s_wo", [16, 128, 4096])
    s_wab = ds("s_wab", [24, 128, 2048])
    s_wpb = ds("s_wpb", [16, 128, 2048])
    s_wqb = ds("s_wqb", [32, 128, 2048])
    s_wzb = ds("s_wzb", [16, 128, 2048])
    s_wbab = ds("s_wbab", [C, 32])
    s_gq = ds("s_gq", [128, 1], F32)

    from contextlib import ExitStack
    with ExitStack() as ctx:
        sb = lambda n, s, d: ctx.enter_context(nc.sbuf_tensor(n, s, d))
        psd = lambda n, s, d: ctx.enter_context(nc.psum_tensor(n, s, d))

        ident_b = sb("ident_b", [128, 128], BF16)
        ident_f = sb("ident_f", [128, 128], F32)
        triu_f = sb("triu_f", [128, 128], F32)
        ones_c = sb("ones_c", [128, 1], BF16)
        ones_r = sb("ones_r", [1, 128], BF16)
        ones_rf = sb("ones_rf", [1, 128], F32)
        ones_cf = sb("ones_cf", [128, 1], F32)
        nw_s = sb("nw_s", [1, 128], F32)
        nw_b = sb("nw_b", [1, 128], BF16)
        NWbc = sb("NWbc", [128, 128], BF16)
        cw_s = sb("cw_s", [128, 128], F32)
        dtb_s = sb("dtb_s", [16, 1], F32)
        alog_s = sb("alog_s", [16, 1], F32)
        negea = sb("negea", [16, 1], F32)
        grow_f = sb("grow_f", [1, 128], F32)
        negrow_f = sb("negrow_f", [1, 128], F32)
        glbrow_f = sb("glbrow_f", [1, 128], F32)
        rowbuf = sb("rowbuf", [1, 512], F32)
        rowbuf2 = sb("rowbuf2", [1, 512], F32)
        rowbuf3 = sb("rowbuf3", [1, 512], F32)
        rowbf = sb("rowbf", [1, 512], BF16)
        rowbfB = sb("rowbfB", [1, 512], BF16)
        scol = sb("scol", [128, 1], F32)
        colGL = sb("colGL", [128, 1], F32)
        colBL = sb("colBL", [128, 1], F32)
        sinv_c = sb("sinv_c", [128, 1], F32)
        acc_c = sb("acc_c", [128, 1], F32)
        acc_c2 = sb("acc_c2", [128, 1], F32)
        gam_r = sb("gam_r", [1, 1], F32)
        c_eps5 = sb("c_eps5", [1, 1], F32)
        c_eps6 = sb("c_eps6", [1, 1], F32)
        c_eps6c = sb("c_eps6c", [128, 1], F32)
        c_one16 = sb("c_one16", [16, 1], F32)
        c_lnq = sb("c_lnq", [1, 1], F32)

        WB = sb("WB", [128, 39936], BF16)
        WF = sb("WF", [128, 19968], F32)
        WW = sb("WW", [128, 11264], BF16)

        GPb = WF[0:16, 0:4096]
        GPa = WF[0:16, 4096:8192]
        GX = WF[0:16, 8192:12288]
        GS2 = WF[0:16, 12288:16384]
        btm = WF[:, 16384:16896]
        gtm = WF[:, 16896:17408]
        Gtm = WF[:, 17408:17920]
        Btm = WF[:, 17920:18432]
        negBtm = WF[:, 18432:18944]
        lnbtm = WF[:, 18944:19456]
        GLBtm = WF[:, 19456:19968]

        PB = [psd(f"PB{i}", [128, 512], F32) for i in range(6)]
        P_ROW = psd("P_ROW", [1, 512], F32)
        P_B = psd("P_B", [128, 512], F32)
        P_SM = PB[4]

        semh = {k: ctx.enter_context(nc.semaphore(f"sem_{k}")) for k in SEMS}
        block = ctx.enter_context(nc.Block())
        ser = Ser()
        c = ser.c

        ser.d(lambda e, h: e.dma_start(out=nw_s[:], in_=nw))
        ser.d(lambda e, h: e.dma_start(out=dtb_s[:], in_=dtb))
        ser.d(lambda e, h: e.dma_start(out=alog_s[:], in_=alog))
        ser.d(lambda e, h: e.dma_start(
            out=cw_s[:, :].rearrange("p (t j) -> p t j", t=32),
            in_=cwm.rearrange("(t p) j -> p t j", p=128)))
        for tile, lower, upper in ((ident_b, True, True), (ident_f, True, True),
                                   (triu_f, False, True)):
            ser.v(lambda e, h, t=tile: e.memset(t[:], 1.0))
            if lower:
                ser.g(lambda e, h, t=tile: e.affine_select(
                    out=t[:], in_=t[:], pattern=[[-1, 128]], compare_op=ALU.is_ge,
                    fill=0.0, base=0, channel_multiplier=1))
            if upper:
                ser.g(lambda e, h, t=tile: e.affine_select(
                    out=t[:], in_=t[:], pattern=[[1, 128]], compare_op=ALU.is_ge,
                    fill=0.0, base=0, channel_multiplier=-1))
        for tile, val in ((ones_c, 1.0), (ones_r, 1.0), (ones_rf, 1.0),
                          (ones_cf, 1.0), (c_eps5, 1e-5), (c_eps6, 1e-6),
                          (c_eps6c, 1e-6), (c_one16, 1.0), (c_lnq, LOG_INV_SQRT_D)):
            ser.v(lambda e, h, t=tile, v=val: e.memset(t[:], v))
        ser.v(lambda e, h: e.tensor_copy(nw_b[:], nw_s[:]))
        ser.t(lambda e, h: e.matmul(P_B[:, 0:128], ones_r[:], nw_b[:], start=True, stop=True))
        ser.v(lambda e, h: e.tensor_copy(NWbc[:], P_B[:, 0:128]))
        ser.s(lambda e, h: e.activation(negea[:], alog_s[:], AF.Exp))
        ser.v(lambda e, h: e.tensor_scalar_mul(negea[:], negea[:], -1.0))
        PC = {}
        for di, (nm, dst, src, nmt) in enumerate(
                (("wab", s_wab, waT, 24), ("wpb", s_wpb, wpT, 16),
                 ("wqb", s_wqb, wqkvT, 32), ("wzb", s_wzb, wzT, 16),
                 ("wo", s_wo, woT, 16))):
            for mt in range(nmt):
                w0 = [("G", c["G"])] if (di == 0 and mt == 0 and c["G"]) else []
                ser.p("gpsimd", lambda e, h, d=dst, s=src, mt=mt: e.dma_start(
                    out=d[mt].rearrange("p (k m) -> p k m", m=128),
                    in_=s.rearrange("(k p) n -> p k n", p=128)[:, :, mt * 128:(mt + 1) * 128]),
                    waits=w0, inc=("LG", 16))
            PC[nm] = c["LG"]
        ser.p("gpsimd", lambda e, h: e.dma_start(out=s_wbab, in_=wbaT),
              inc=("LG", 16))
        PC["wbab"] = c["LG"]
        for (full, half) in ((w1T, s_w1h), (w2T, s_w2h), (w3T, s_w3h)):
            rows = half.shape[0]
            ser.p("gpsimd", lambda e, h, full=full, half=half, rows=rows:
                  e.dma_start(out=half,
                              in_=full[bass.DynSlice(h["par_g"] * rows, rows), :]),
                  inc=("LG", 16))
        QCP = [c["LG"]]

        def emit_ln(src, dst, ntok):
            ser.begin()
            xf2 = [WF[:, 0:8192], WF[:, 8192:16384]]
            ab2 = [WF[:, 16384:16896], WF[:, 16896:17408]]
            bb2 = [WF[:, 17408:17920], WF[:, 17920:18432]]
            tm2 = [WF[:, 18432:18944], WF[:, 18944:19456]]
            xb2 = [WB[:, 0:8192], WB[:, 8192:16384]]
            xs2 = [WB[:, 16384:24576], WB[:, 24576:32768]]
            st = {}
            prow_free = {}
            pb_free = {}
            nb = ntok // 512
            for blk in range(nb):
                pr = blk % 2
                xf, ab, bb, tm = xf2[pr], ab2[pr], bb2[pr], tm2[pr]
                xb, xs = xb2[pr], xs2[pr]
                w0 = blk * 512
                w = [("S", st[blk - 2])] if blk >= 2 else []
                ser.p("sync", lambda e, h, w0=w0, xf=xf: e.dma_start(
                    out=xf.rearrange("p (k n) -> p k n", k=KT),
                    in_=src.rearrange("(k p) t -> p k t", p=128)[:, :, w0:w0 + 512]),
                    waits=w, inc=("L", 16))
                ldb = c["L"]
                ser.p("vector", lambda e, h, xb=xb, xf=xf: e.tensor_copy(xb[:], xf[:]),
                      waits=[("L", ldb)], inc=("F", 1))
                f_cast = c["F"]
                ser.p("scalar", lambda e, h, xs=xs, xf=xf: e.activation(
                    xs[:], xf[:], AF.Square), waits=[("L", ldb)], inc=("FS", 1))
                fs_sq = c["FS"]
                for kt in range(KT):
                    wm = []
                    if kt == 0:
                        wm = [("F", max(f_cast, prow_free.get(blk - 1, 0)))]
                    ser.p("tensor", lambda e, h, kt=kt, xb=xb: e.matmul(
                        P_ROW[:, :], ones_c[:], xb[:, kt * 512:(kt + 1) * 512],
                        start=(kt == 0), stop=(kt == KT - 1)),
                        waits=wm, inc=(("M", 1) if kt == KT - 1 else None))
                m1 = c["M"]
                ser.p("vector", lambda e, h: e.tensor_scalar_mul(
                    rowbuf[:], P_ROW[:, :], 1.0 / C),
                    waits=[("M", m1), ("F", c["F"])], inc=("F", 1))
                f2 = c["F"]
                for kt in range(KT):
                    wm = []
                    if kt == 0:
                        wm = [("FS", fs_sq), ("F", f2)]
                    ser.p("tensor", lambda e, h, kt=kt, xs=xs: e.matmul(
                        P_ROW[:, :], ones_c[:], xs[:, kt * 512:(kt + 1) * 512],
                        start=(kt == 0), stop=(kt == KT - 1)),
                        waits=wm, inc=(("M", 1) if kt == KT - 1 else None))
                m2 = c["M"]
                ser.p("vector", lambda e, h: e.tensor_scalar_mul(
                    rowbuf2[:], P_ROW[:, :], 1.0 / C),
                    waits=[("M", m2), ("F", c["F"])], inc=("F", 1))
                prow_free[blk] = c["F"]
                ser.p("vector", lambda e, h: e.tensor_mul(rowbuf3[:], rowbuf[:], rowbuf[:]),
                      waits=[("F", c["F"])], inc=("F", 1))
                ser.p("vector", lambda e, h: e.tensor_sub(rowbuf2[:], rowbuf2[:], rowbuf3[:]),
                      waits=[("F", c["F"])], inc=("F", 1))
                f5 = c["F"]
                ser.p("scalar", lambda e, h: e.activation(
                    rowbuf2[:], rowbuf2[:], AF.Ln, bias=c_eps5[:]),
                    waits=[("F", f5), ("FS", c["FS"])], inc=("FS", 1))
                ser.p("scalar", lambda e, h: e.activation(
                    rowbuf2[:], rowbuf2[:], AF.Exp, scale=-0.5),
                    waits=[("FS", c["FS"])], inc=("FS", 1))
                fs3 = c["FS"]
                ser.p("vector", lambda e, h: e.tensor_copy(rowbf[:], rowbuf2[:]),
                      waits=[("FS", fs3), ("F", c["F"])], inc=("F", 1))
                f6 = c["F"]
                ser.p("tensor", lambda e, h: e.matmul(
                    P_B[:, :], ones_r[:], rowbf[:], start=True, stop=True),
                    waits=[("F", max(f6, pb_free.get(blk - 1, 0)))], inc=("M", 1))
                m3 = c["M"]
                ser.p("vector", lambda e, h, ab=ab: e.tensor_copy(ab[:], P_B[:, :]),
                      waits=[("M", m3), ("F", c["F"])], inc=("F", 1))
                f7 = c["F"]
                ser.p("vector", lambda e, h: e.tensor_mul(rowbuf3[:], rowbuf[:], rowbuf2[:]),
                      waits=[("F", c["F"])], inc=("F", 1))
                ser.p("vector", lambda e, h: e.tensor_scalar_mul(rowbuf3[:], rowbuf3[:], -1.0),
                      waits=[("F", c["F"])], inc=("F", 1))
                ser.p("vector", lambda e, h: e.tensor_copy(rowbf[:], rowbuf3[:]),
                      waits=[("F", c["F"])], inc=("F", 1))
                f10 = c["F"]
                ser.p("tensor", lambda e, h: e.matmul(
                    P_B[:, :], ones_r[:], rowbf[:], start=True, stop=True),
                    waits=[("F", max(f10, f7))], inc=("M", 1))
                m4 = c["M"]
                ser.p("vector", lambda e, h, bb=bb: e.tensor_copy(bb[:], P_B[:, :]),
                      waits=[("M", m4), ("F", c["F"])], inc=("F", 1))
                pb_free[blk] = c["F"]
                for kt in range(KT):
                    sl = slice(kt * 512, (kt + 1) * 512)
                    ser.p("vector", lambda e, h, sl=sl, tm=tm, xf=xf, ab=ab:
                          e.tensor_mul(tm[:], xf[:, sl], ab[:]),
                          waits=([("F", c["F"])] if kt == 0 else []), inc=("F", 1))
                    ser.p("vector", lambda e, h, sl=sl, tm=tm, xs=xs, bb=bb:
                          e.tensor_add(xs[:, sl], tm[:], bb[:]), inc=("F", 1))
                f12 = c["F"]
                ser.p("sync", lambda e, h, w0=w0, xs=xs: e.dma_start(
                    out=dst.rearrange("(k p) t -> p k t", p=128)[:, :, w0:w0 + 512],
                    in_=xs.rearrange("p (k n) -> p k n", k=KT)),
                    waits=[("F", f12)], inc=("S", 16))
                st[blk] = c["S"]
            ser.end()

        emit_ln(xT, s_ln1, T)

        def sec_qk():
            ser.begin()
            eb0 = c["E"]
            WR = WB[:, 0:32768]
            rb = [WF[:, 0:4096].bitcast(BF16), WF[:, 4096:8192].bitcast(BF16)]
            obufs = [WB[:, 32768:33280], WB[:, 33280:33792]]
            st_box = {}
            mm_box = {}
            NMT = 16
            for mt in range(NMT):
                ser.p("scalar", lambda e, h, mt=mt: e.dma_start(
                    out=WR[:, mt * 2048:(mt + 1) * 2048], in_=s_wab[mt]),
                    waits=([("LG", PC["wab"])] if mt == 0 else []),
                    inc=("LS", 16))
            w_all = c["LS"]
            for n in range(8):
                rbuf = rb[n % 2]
                w = [("M", mm_box[(n - 2) * NMT + NMT - 1])] if n >= 2 else []
                ser.p("sync", lambda e, h, n=n, rbuf=rbuf: e.dma_start(
                    out=rbuf.rearrange("p (k t) -> p k t", k=KT),
                    in_=s_ln1.rearrange("(k p) t -> p k t", p=128)[:, :, n * 512:(n + 1) * 512]),
                    waits=w, inc=("L", 16))
                r_abs = c["L"]
                for mt in range(NMT):
                    u = n * NMT + mt
                    wt = WR[:, mt * 2048:(mt + 1) * 2048]
                    bank = PB[u % 3]
                    for kt in range(KT):
                        ww = []
                        if kt == 0:
                            ww = [("LS", w_all), ("L", r_abs)]
                            if u >= 3:
                                ww.append(("E", eb0 + u - 2))
                        ser.p("tensor", lambda e, h, kt=kt, wt=wt, bank=bank, rbuf=rbuf:
                              e.matmul(bank[:, :], wt[:, kt * 128:(kt + 1) * 128],
                                       rbuf[:, kt * 512:(kt + 1) * 512],
                                       start=(kt == 0), stop=(kt == KT - 1)),
                              waits=ww, inc=(("M", 1) if kt == KT - 1 else None))
                    mm_box[u] = c["M"]
                    ob = obufs[u % 2]
                    wx = [("M", mm_box[u])]
                    if u >= 2:
                        wx.append(("S", st_box[u - 2]))
                    ser.p("vector", lambda e, h, ob=ob, bank=bank: e.tensor_copy(ob[:], bank[:, :]),
                          waits=wx, inc=("E", 1))
                    e_abs = c["E"]
                    dstd = s_q if mt < 8 else s_k
                    r0 = (mt % 8) * 128
                    ser.p("sync", lambda e, h, ob=ob, dstd=dstd, r0=r0, n=n: e.dma_start(
                        out=dstd[r0:r0 + 128, n * 512:(n + 1) * 512], in_=ob[:]),
                        waits=[("E", e_abs)], inc=("S", 16))
                    st_box[u] = c["S"]
            ser.end()

        sec_qk()

        def sec_v():
            ser.begin()
            eb0 = c["E"]
            wv = WB[:, 0:KT * 1024]
            for vmt in range(8):
                ser.p("sync", lambda e, h, vmt=vmt: e.dma_start(
                    out=wv.rearrange("p (k n) -> p k n", k=KT)[:, :, vmt * 128:(vmt + 1) * 128],
                    in_=s_wab[16 + vmt].rearrange("p (k m) -> p k m", m=128)),
                    waits=([("LG", PC["wab"])] if vmt == 0 else []),
                    inc=("L", 16))
            wv_abs = c["L"]
            lts = [WB[:, 16384:18432], WB[:, 18432:20480]]
            obufs = [WB[:, 20480:20992], WB[:, 20992:21504]]
            st_box = {}
            mm_box = {}
            for tm in range(T // 128):
                lt = lts[tm % 2]
                w = [("M", mm_box[(tm - 2) * 2 + 1])] if tm >= 2 else []
                ser.p("scalar", lambda e, h, tm=tm, lt=lt: e.dma_start(
                    out=lt.rearrange("p (k n) -> p k n", k=KT),
                    in_=s_ln1.rearrange("(k p) t -> p k t", p=128)[:, :, tm * 128:(tm + 1) * 128]),
                    waits=w, inc=("LS", 16))
                lt_a = c["LS"]
                for n in range(2):
                    u = tm * 2 + n
                    bank = PB[u % 3]
                    for kt in range(KT):
                        ww = []
                        if kt == 0:
                            ww = [("L", wv_abs), ("LS", lt_a)]
                            if u >= 3:
                                ww.append(("E", eb0 + u - 2))
                        ser.p("tensor", lambda e, h, kt=kt, n=n, lt=lt, bank=bank:
                              e.matmul(bank[:, :], lt[:, kt * 128:(kt + 1) * 128],
                                       wv[:, kt * 1024 + n * 512: kt * 1024 + (n + 1) * 512],
                                       start=(kt == 0), stop=(kt == KT - 1)),
                              waits=ww, inc=(("M", 1) if kt == KT - 1 else None))
                    mm_box[u] = c["M"]
                    ob = obufs[u % 2]
                    wx = [("M", mm_box[u])]
                    if u >= 2:
                        wx.append(("S", st_box[u - 2]))
                    ser.p("vector", lambda e, h, ob=ob, bank=bank: e.tensor_copy(ob[:], bank[:, :]),
                          waits=wx, inc=("E", 1))
                    e_abs = c["E"]
                    ser.p("sync", lambda e, h, ob=ob, tm=tm, n=n: e.dma_start(
                        out=s_vt[tm * 128:(tm + 1) * 128, n * 512:(n + 1) * 512], in_=ob[:]),
                        waits=[("E", e_abs)], inc=("S", 16))
                    st_box[u] = c["S"]
            ser.end()

        sec_v()

        g3r = WF[0:4, 0:256]
        g3c = WF[0:4, 256:257]
        sinv3 = sb("sinv3", [128, 4], F32)
        QMATS = [(s_w1h, qh1, 8, 5632, 0), (s_w2h, qh2d, 8, 5632, 1),
                 (s_w3h, qh3, 22, 2048, 2)]

        def make_p1_steps():
            wt2 = [WF[:, 1024:1024 + 5632], WF[:, 1024 + 5632:1024 + 11264]]
            ld = {}
            rd = {}
            steps = []
            g = 0
            for wsrc, qdst, nh, cols, j in QMATS:
                for i in range(nh):
                    def em(g=g, i=i, nh=nh, wsrc=wsrc, cols=cols, j=j):
                        w = [("F", rd[g - 2])] if g >= 2 else []
                        if g == 0:
                            w.append(("LG", QCP[0]))
                        ser.p("sync", lambda e, h, i=i, wsrc=wsrc,
                              wt=wt2[g % 2][:, 0:cols]: e.dma_start(
                                  out=wt, in_=wsrc[i * 128:(i + 1) * 128, :]),
                              waits=w, inc=("L", 16))
                        ld[g] = c["L"]
                        ser.p("vector", lambda e, h, wt=wt2[g % 2][:, 0:cols]:
                              e.reduce_sum(acc_c2[:], wt, axis=AX.X,
                                           apply_absolute_value=True),
                              waits=[("L", ld[g])], inc=("F", 1))
                        if i == 0:
                            ser.p("vector", lambda e, h: e.tensor_copy(
                                acc_c[:], acc_c2[:]),
                                waits=[("F", c["F"])], inc=("F", 1))
                        else:
                            ser.p("vector", lambda e, h: e.tensor_add(
                                acc_c[:], acc_c[:], acc_c2[:]),
                                waits=[("F", c["F"])], inc=("F", 1))
                        rd[g] = c["F"]
                        if i == nh - 1:
                            ser.p("sync", lambda e, h, j=j: e.dma_start(
                                out=cc4_in[:, j:j + 1], in_=acc_c[:],
                                allow_slow_non_contiguous=True),
                                waits=[("F", rd[g])], inc=("L", 16))
                    steps.append(em)
                    g += 1
            return steps

        p1_q = make_p1_steps()

        def sec_attn():
            ser.begin()
            qb = [WB[:, 0:4096], WB[:, 4096:8192]]
            kb = [WB[:, 8192:12288], WB[:, 12288:16384]]
            vb = [WB[:, 16384:20480], WB[:, 20480:24576]]
            e_s = [WB[:, 24576:25088], WB[:, 25088:25600],
                   WB[:, 25600:26112], WB[:, 26112:26624]]
            ao_s = [WB[:, 26624:27136], WB[:, 27136:27648]]
            bc_f = [WF[:, 0:512], WF[:, 512:1024]]
            hl_abs = {}
            pe_last = {}

            def emit_loads(hh, w):
                ser.p("sync", lambda e, h, hh=hh, q=qb[hh % 2]: e.dma_start(
                    out=q[:, :], in_=s_q[hh * 128:(hh + 1) * 128, :]), waits=w, inc=("L", 16))
                ser.p("sync", lambda e, h, hh=hh, k=kb[hh % 2]: e.dma_start(
                    out=k[:, :], in_=s_k[hh * 128:(hh + 1) * 128, :]), inc=("L", 16))
                ser.p("sync", lambda e, h, hh=hh, v=vb[hh % 2]: e.dma_start(
                    out=v.rearrange("p (b n) -> p b n", b=32),
                    in_=s_vt.rearrange("(b p) v -> p b v", p=128)[:, :, hh * 128:(hh + 1) * 128]),
                    inc=("L", 16))
                hl_abs[hh] = c["L"]

            emit_loads(0, [])
            emit_loads(1, [])
            exps = []
            gs_box = [0]
            buf_last_rs = {0: 0, 1: 0, 2: 0, 3: 0}
            ge = 0
            stt_abs = {}
            recip_abs = {}
            tail_abs = {}
            bc_abs = {}
            bcev_abs = {}
            st_abs = {}
            for hh in range(NHA):
                q, k, v = qb[hh % 2], kb[hh % 2], vb[hh % 2]
                for qg in range(8):
                    u = hh * 8 + qg
                    nkb = 4 * qg + 4
                    pv_a = rs_a = 0
                    pend = None

                    def emit_pv(eb, kbi, x_a):
                        nonlocal pv_a, rs_a
                        w = [x_a]
                        if kbi == 0 and u >= 1:
                            w.append(("E", stt_abs[u - 1]))
                        ser.p("tensor", lambda e, h, kbi=kbi, v=v, es=e_s[eb]:
                              e.matmul(PB[0][:, :], v[:, kbi * 128:(kbi + 1) * 128], es[:],
                                       start=(kbi == 0), stop=(kbi == nkb - 1)),
                              waits=w, inc=("M", 1))
                        pv_a = c["M"]
                        w = []
                        if kbi == 0 and u >= 1:
                            w.append(("F", tail_abs[u - 1]))
                        ser.p("tensor", lambda e, h, kbi=kbi, es=e_s[eb]:
                              e.matmul(P_ROW[:, :], ones_c[:], es[:],
                                       start=(kbi == 0), stop=(kbi == nkb - 1)),
                              waits=w, inc=("M", 1))
                        rs_a = c["M"]
                        buf_last_rs[eb] = rs_a

                    for kbi in range(nkb):
                        sbank = PB[1 + gs_box[0] % 3]
                        gs_box[0] += 1
                        w = [("L", hl_abs[hh])]
                        if len(exps) >= 3:
                            w.append(exps[-3])
                        ser.p("tensor", lambda e, h, kbi=kbi, qg=qg, sbank=sbank, k=k, q=q:
                              e.matmul(sbank[:, :], k[:, kbi * 128:(kbi + 1) * 128],
                                       q[:, qg * 512:(qg + 1) * 512], start=True, stop=True),
                              waits=w, inc=("M", 1))
                        sc_a = c["M"]
                        eb = ge % 4
                        ge += 1
                        w = [("M", sc_a)]
                        if buf_last_rs[eb]:
                            w.append(("M", buf_last_rs[eb]))
                        ser.p("scalar", lambda e, h, es=e_s[eb], sbank=sbank:
                              e.activation(es[:], sbank[:, :], AF.Exp, scale=INV_SQRT_D),
                              waits=w, inc=("FS", 1))
                        x_a = ("FS", c["FS"])
                        if kbi >= 4 * qg:
                            base = 128 * (4 * qg - kbi)
                            ser.p("gpsimd", lambda e, h, es=e_s[eb], base=base:
                                  e.affine_select(
                                      out=es.rearrange("p (q y) -> p q y", q=4),
                                      in_=es.rearrange("p (q y) -> p q y", q=4),
                                      pattern=[[128, 4], [1, 128]], compare_op=ALU.is_ge,
                                      fill=0.0, base=base, channel_multiplier=-1),
                                  waits=[x_a], inc=("FG", 1))
                            x_a = ("FG", c["FG"])
                        exps.append(x_a)
                        if pend is not None:
                            emit_pv(*pend)
                        pend = (eb, kbi, x_a)
                    emit_pv(*pend)
                    ser.p("vector", lambda e, h: e.reciprocal(rowbuf[:], P_ROW[:, :]),
                          waits=[("M", rs_a)] + ([("M", bc_abs[u - 1])] if u >= 1 else []),
                          inc=("F", 1))
                    recip_abs[u] = c["F"]
                    ser.p("vector", lambda e, h: e.tensor_copy(rowbfB[:], rowbuf[:]),
                          waits=[("F", c["F"])], inc=("F", 1))
                    cast_a = c["F"]
                    wbc = [("F", cast_a)]
                    if u >= 1:
                        wbc.append(("FS", bcev_abs[u - 1]))
                    ser.p("tensor", lambda e, h: e.matmul(P_B[:, :], ones_r[:], rowbfB[:],
                                                          start=True, stop=True),
                          waits=wbc, inc=("M", 1))
                    bc_a = c["M"]
                    bc_abs[u] = bc_a
                    ser.p("scalar", lambda e, h, bf=bc_f[u % 2]: e.copy(bf[:], P_B[:, :]),
                          waits=[("M", bc_a)], inc=("FS", 1))
                    bcev_abs[u] = c["FS"]
                    w = [("FS", bcev_abs[u]), ("M", pv_a)]
                    if u >= 2:
                        w.append(("S", st_abs[u - 2]))
                    ser.p("vector", lambda e, h, ao=ao_s[u % 2], bf=bc_f[u % 2]:
                          e.scalar_tensor_tensor(ao[:], PB[0][:, :], 1.0, bf[:],
                                                 ALU.mult, ALU.mult),
                          waits=w, inc=("E", 1))
                    stt_abs[u] = c["E"]
                    tail_abs[u] = recip_abs[u]
                    ser.p("sync", lambda e, h, ao=ao_s[u % 2], hh=hh, qg=qg: e.dma_start(
                        out=cc1_in[hh * 128:(hh + 1) * 128, qg * 512:(qg + 1) * 512],
                        in_=ao[:]), waits=[("E", stt_abs[u])], inc=("S", 16))
                    st_abs[u] = c["S"]
                    if p1_q:
                        p1_q.pop(0)()
                pe_last[hh] = c["M"]
                if hh + 2 < NHA:
                    emit_loads(hh + 2, [("M", pe_last[hh])])
            while p1_q:
                p1_q.pop(0)()
            ser.end()

        sec_attn()

        ser.cc(lambda e, h: e.collective_compute(
            "AllGather", ALU.bypass,
            replica_groups=[[0, 1], [2, 3], [4, 5], [6, 7]],
            ins=[cc1_in.opt()], outs=[cc1_out.opt()]))
        ser.cc(lambda e, h: e.collective_compute(
            "AllGather", ALU.bypass,
            replica_groups=[[0, 1], [2, 3], [4, 5], [6, 7]],
            ins=[cc4_in.opt()], outs=[cc4_out.opt()]))
        ser.d(lambda e, h: e.dma_start(
            out=g3r[0:3, :], in_=cc4_out.rearrange("p a -> a p")[0:3, :],
            allow_slow_non_contiguous=True))
        ser.v(lambda e, h: e.reduce_sum(g3c[0:3, :], g3r[0:3, :], axis=AX.X))
        ser.v(lambda e, h: e.tensor_scalar(
            g3c[0:3, :], g3c[0:3, :], 1.0 / (C * HFF), 1e-8, ALU.mult, ALU.add))
        ser.v(lambda e, h: e.reciprocal(g3c[0:3, :], g3c[0:3, :]))
        ser.d(lambda e, h: e.dma_start(out=s_gq3[0:3, :], in_=g3c[0:3, :]))
        for j in range(3):
            ser.d(lambda e, h, j=j: e.dma_start(
                out=sinv3[:, j:j + 1], in_=s_gq3[j:j + 1, :].partition_broadcast(128)))

        def sec_nproj(kk, w_segs, rhs_dram, old_dram, new_dram, lgw=None):
            ser.begin()
            wta = WW[:, 0:kk * 128]
            wtb = WW[:, kk * 128:2 * kk * 128]
            hb = WB[:, 0:kk * 512]
            obufs = [WF[:, 0:512], WF[:, 512:1024]]
            olbufs = [WF[:, 1024:1536], WF[:, 1536:2048]]
            st_box = {}
            mm_box = {}
            ev_box = {}
            for n in range(4):
                w = [("M", mm_box[(n - 1) * 16 + 15])] if n >= 1 else []
                ser.p("sync", lambda e, h, n=n: e.dma_start(
                    out=hb.rearrange("p (k t) -> p k t", k=kk),
                    in_=rhs_dram.rearrange("(k p) t -> p k t", p=128)[:, :, n * 512:(n + 1) * 512]),
                    waits=w, inc=("L", 16))
                hb_abs = c["L"]
                for mt in range(16):
                    u = n * 16 + mt
                    wt = (wta, wtb)[u % 2]
                    wl = [("M", mm_box[u - 2])] if u >= 2 else []
                    co = 0
                    for si, (wsrc3, cw) in enumerate(w_segs):
                        wle = list(wl) if si == 0 else []
                        if si == 0 and u == 0 and lgw is not None:
                            wle.append(("LG", lgw))
                        ser.p("scalar", lambda e, h, mt=mt, wt=wt, co=co,
                              wsrc3=wsrc3, cw=cw: e.dma_start(
                                  out=wt[:, co:co + cw], in_=wsrc3[mt]),
                              waits=wle, inc=("LS", 16))
                        co += cw
                    ld_a = c["LS"]
                    olb = olbufs[u % 2]
                    wl2 = [("E", ev_box[u - 2])] if u >= 2 else []
                    ser.p("gpsimd", lambda e, h, olb=olb, mt=mt, n=n: e.dma_start(
                        out=olb[:], in_=old_dram[mt * 128:(mt + 1) * 128, n * 512:(n + 1) * 512]),
                        waits=wl2, inc=("LG", 16))
                    lo_a = c["LG"]
                    bank = PB[u % 3]
                    for kt in range(kk):
                        ww = []
                        if kt == 0:
                            ww = [("L", hb_abs), ("LS", ld_a)]
                            if u >= 3:
                                ww.append(("E", ev_box[u - 3]))
                        ser.p("tensor", lambda e, h, kt=kt, wt=wt, bank=bank:
                              e.matmul(bank[:, :], wt[:, kt * 128:(kt + 1) * 128],
                                       hb[:, kt * 512:(kt + 1) * 512],
                                       start=(kt == 0), stop=(kt == kk - 1)),
                              waits=ww, inc=(("M", 1) if kt == kk - 1 else None))
                    mm_box[u] = c["M"]
                    ob = obufs[u % 2]
                    w = [("M", mm_box[u]), ("LG", lo_a)]
                    if u >= 2:
                        w.append(("S", st_box[u - 2]))
                    ser.p("vector", lambda e, h, ob=ob, olb=olb, bank=bank: e.tensor_add(
                        ob[:], olb[:], bank[:, :]), waits=w, inc=("E", 1))
                    ev_box[u] = c["E"]
                    ser.p("sync", lambda e, h, ob=ob, mt=mt, n=n: e.dma_start(
                        out=new_dram[mt * 128:(mt + 1) * 128, n * 512:(n + 1) * 512],
                        in_=ob[:]), waits=[("E", ev_box[u])], inc=("S", 16))
                    st_box[u] = c["S"]
            ser.end()

        ser.d(lambda e, h: e.dma_start(out=s_ao, in_=cc1_out[:, bass.ts(h["par"], TOWN)]))
        ser.d(lambda e, h: e.dma_start(out=s_xo, in_=xT[:, bass.ts(h["par"], TOWN)]))
        sec_nproj(KT, [(s_wpb, 2048)], s_ao, s_xo, resid1, lgw=PC["wpb"])

        emit_ln(resid1, s_ln2, TOWN)



        def emit_quant_p2():
            ser.begin()
            wt2 = [WF[:, 1024:1024 + 5632], WF[:, 1024 + 5632:1024 + 11264]]
            qt2 = [WB[:, 0:5632], WB[:, 5632:11264]]
            qv = {}
            st = {}
            g = 0
            for wsrc, qdst, nh, cols, j in QMATS:
                qdst3 = qdst.rearrange("a (p m) -> a p m", p=128)
                for i in range(nh):
                    w = [("F", qv[g - 2])] if g >= 2 else []
                    ser.p("sync", lambda e, h, i=i, wsrc=wsrc,
                          wt=wt2[g % 2][:, 0:cols]: e.dma_start(
                              out=wt, in_=wsrc[i * 128:(i + 1) * 128, :]),
                          waits=w, inc=("L", 16))
                    lk = c["L"]
                    wt = wt2[g % 2][:, 0:cols]
                    qt = qt2[g % 2][:, 0:cols]
                    wi = WB[:, 11264 + (g % 2) * 11264:
                            11264 + (g % 2) * 11264 + 2 * cols].bitcast(I32)
                    ser.p("scalar", lambda e, h, wi=wi, wt=wt, j=j: e.activation(
                        wi, wt, AF.Copy, scale=sinv3[:, j:j + 1]),
                        waits=[("L", lk)], inc=("FS", 1))
                    ser.p("vector", lambda e, h, wi=wi: e.tensor_scalar(
                        wi, wi, 1, -1, ALU.min, ALU.max),
                        waits=[("FS", c["FS"]), ("F", c["F"])], inc=("F", 1))
                    wv = [("F", c["F"])]
                    if g >= 2:
                        wv.append(("S", st[g - 2]))
                    ser.p("vector", lambda e, h, wi=wi, qt=qt: e.tensor_copy(qt, wi),
                          waits=wv, inc=("F", 1))
                    qv[g] = c["F"]
                    ser.p("sync", lambda e, h, i=i, qt=qt, qdst3=qdst3: e.dma_start(
                        out=qdst3[:, :, i * 128:(i + 1) * 128].rearrange("a p m -> p a m"),
                        in_=qt.rearrange("p (a m) -> p a m", m=128)),
                        waits=[("F", qv[g])], inc=("S", 16))
                    st[g] = c["S"]
                    g += 1
            ser.end()

        emit_quant_p2()
        gq0 = c["G"]
        for ci, (qi, qo) in enumerate(((qh1, qg1), (qh2d, qg2), (qh3, qg3))):
            ser.p("gpsimd", lambda e, h, qi=qi, qo=qo: e.collective_compute(
                "AllGather", ALU.bypass,
                replica_groups=[[0, 1], [2, 3], [4, 5], [6, 7]],
                ins=[qi.opt()], outs=[qo.opt()]),
                waits=([("G", gq0)] if ci == 0 and gq0 else []), inc=("FG", 1))
        ser.p("sync", lambda e, h: e.nop(), waits=[("FG", c["FG"])], inc=("G", 1))

        def sec_ffn_hidden(ln_dram):
            ser.begin()
            rr = WB[:, 0:KT * 2048]
            ser.p("sync", lambda e, h: e.dma_start(
                out=rr.rearrange("p (k t) -> p k t", k=KT),
                in_=ln_dram.rearrange("(k p) t -> p k t", p=128)), inc=("L", 16))
            r_abs = c["L"]
            sigs = [WB[:, 32768:33280], WB[:, 33280:33792]]
            hids = [WB[:, 33792:34304], WB[:, 34304:34816]]
            st_box = {}
            mm_box = {}
            sig_box = {}
            stt_box = {}
            for u in range(44 * 4):
                hm, n = divmod(u, 4)
                wt1 = (WW[:, 0:2048], WW[:, 2048:4096])[hm % 2]
                wt2 = (WW[:, 4096:6144], WW[:, 6144:8192])[hm % 2]
                if n == 0:
                    wl = [("M", mm_box[(hm - 2) * 4 + 3])] if hm >= 2 else []
                    qg1v = qg1.rearrange("a (p m) -> a p m", p=128)
                    qg2v = qg2.rearrange("a (p m) -> a p m", p=128)
                    ser.p("sync", lambda e, h, hm=hm, wt1=wt1, qg1v=qg1v: e.dma_start(
                        out=wt1[:, 0:1024], in_=qg1v[hm]),
                        waits=wl, inc=("L", 16))
                    ser.p("sync", lambda e, h, hm=hm, wt1=wt1, qg1v=qg1v: e.dma_start(
                        out=wt1[:, 1024:2048], in_=qg1v[44 + hm]),
                        inc=("L", 16))
                    ser.p("sync", lambda e, h, hm=hm, wt2=wt2, qg2v=qg2v: e.dma_start(
                        out=wt2[:, 0:1024], in_=qg2v[hm]),
                        inc=("L", 16))
                    ser.p("sync", lambda e, h, hm=hm, wt2=wt2, qg2v=qg2v: e.dma_start(
                        out=wt2[:, 1024:2048], in_=qg2v[44 + hm]),
                        inc=("L", 16))
                ld_a = c["L"]
                b1, b2 = PB[(u % 3) * 2], PB[(u % 3) * 2 + 1]
                for kt in range(KT):
                    ww = []
                    if kt == 0:
                        ww = [("L", max(r_abs, ld_a))]
                        if u >= 3:
                            ww.append(("FS", sig_box[u - 3]))
                    ser.p("tensor", lambda e, h, kt=kt, wt1=wt1, b1=b1, n=n:
                          e.matmul(b1[:, :], wt1[:, kt * 128:(kt + 1) * 128],
                                   rr[:, kt * 2048 + n * 512: kt * 2048 + (n + 1) * 512],
                                   start=(kt == 0), stop=(kt == KT - 1)),
                          waits=ww, inc=None)
                for kt in range(KT):
                    ww = []
                    if kt == 0 and u >= 3:
                        ww = [("E", stt_box[u - 3])]
                    ser.p("tensor", lambda e, h, kt=kt, wt2=wt2, b2=b2, n=n:
                          e.matmul(b2[:, :], wt2[:, kt * 128:(kt + 1) * 128],
                                   rr[:, kt * 2048 + n * 512: kt * 2048 + (n + 1) * 512],
                                   start=(kt == 0), stop=(kt == KT - 1)),
                          waits=ww, inc=(("M", 1) if kt == KT - 1 else None))
                mm_box[u] = c["M"]
                sig = sigs[u % 2]
                hid = hids[u % 2]
                ser.p("scalar", lambda e, h, sig=sig, b1=b1: e.activation(
                    sig[:], b1[:, :], AF.Sigmoid), waits=[("M", mm_box[u])], inc=("FS", 1))
                sig_box[u] = c["FS"]
                w = [("FS", sig_box[u])]
                if u >= 2:
                    w.append(("S", st_box[u - 2]))
                ser.p("vector", lambda e, h, hid=hid, sig=sig, b2=b2:
                      e.scalar_tensor_tensor(hid[:], b2[:, :], 1.0, sig[:],
                                             ALU.mult, ALU.mult),
                      waits=w, inc=("E", 1))
                stt_box[u] = c["E"]
                ser.p("sync", lambda e, h, hid=hid, hm=hm, n=n: e.dma_start(
                    out=s_hid[hm * 128:(hm + 1) * 128, n * 512:(n + 1) * 512], in_=hid[:]),
                    waits=[("E", stt_box[u])], inc=("S", 16))
                st_box[u] = c["S"]
            ser.end()

        sec_ffn_hidden(s_ln2)
        qg3v = qg3.rearrange("a (p m) -> a p m", p=128)
        sec_nproj(HFF // 128, [(qg3v[0:KT], 2816), (qg3v[KT:2 * KT], 2816)],
                  s_hid, resid1, resid2)

        emit_ln(resid2, cc2_in, TOWN)
        ser.cc(lambda e, h: e.collective_compute(
            "AllGather", ALU.bypass,
            replica_groups=[[0, 1], [2, 3], [4, 5], [6, 7]],
            ins=[cc2_in.opt()], outs=[cc2_out.opt()]))
        for r in range(2):
            ser.d(lambda e, h, r=r: e.dma_start(
                out=s_ln3g[:, r * TOWN:(r + 1) * TOWN], in_=cc2_out[r * C:(r + 1) * C, :]))

        def sec_qkvconv():
            ser.begin()
            RRA = WB[:, 0:32768]
            wts = [WW[:, 0:2048], WW[:, 2048:4096]]
            mxp = ([WB[:, 32768 + i * 2064:32768 + (i + 1) * 2064] for i in range(3)]
                   + [WF[:, 4096:5128].bitcast(BF16)])
            t_a = WF[:, 0:2048]
            t_b = WF[:, 2048:4096]
            mm_box = {}
            ev_box = {}
            f_ch = {}
            s_out = {}
            r_a = [0, 0]

            def emit_acts(hh):
                t0 = hh * 2048
                w = [("M", mm_box[31])] if hh == 1 else []
                ser.p("sync", lambda e, h, t0=t0: e.dma_start(
                    out=RRA.rearrange("p (k t) -> p k t", k=KT),
                    in_=s_ln3g.rearrange("(k p) t -> p k t", p=128)[:, :, t0:t0 + 2048]),
                    waits=w, inc=("L", 16))
                r_a[hh] = c["L"]

            def emit_proj(u):
                hh, ct = u // 32, u % 32
                wt = wts[u % 2]
                wl = [("M", mm_box[u - 2])] if u >= 2 else []
                if u == 0:
                    wl = wl + [("LG", PC["wqb"])]
                ser.p("scalar", lambda e, h, ct=ct, wt=wt: e.dma_start(
                    out=wt[:], in_=s_wqb[ct]),
                    waits=wl, inc=("LS", 16))
                w_a = c["LS"]
                for n in range(4):
                    v = u * 4 + n
                    bank = PB[v % 4]
                    for kt in range(KT):
                        ww = []
                        if kt == 0:
                            ww = [("LS", w_a), ("L", r_a[hh])]
                            if v >= 4:
                                ww.append(("FS", ev_box[v - 4]))
                            if n == 0 and u >= 4:
                                ww.append(f_ch[u - 4])
                        ser.p("tensor", lambda e, h, kt=kt, wt=wt, bank=bank, n=n:
                              e.matmul(bank[:, :], wt[:, kt * 128:(kt + 1) * 128],
                                       RRA[:, kt * 2048 + n * 512:kt * 2048 + (n + 1) * 512],
                                       start=(kt == 0), stop=(kt == KT - 1)),
                              waits=ww, inc=(("M", 1) if kt == KT - 1 else None))
                    mm_box[u] = c["M"]
                    ser.p("scalar", lambda e, h, mx=mxp[u % 4], n=n, bank=bank:
                          e.copy(mx[:, 16 + n * 512:16 + (n + 1) * 512], bank[:, :]),
                          waits=[("M", mm_box[u])], inc=("FS", 1))
                    ev_box[v] = c["FS"]

            def emit_chain(u):
                hh, ct = u // 32, u % 32
                t0 = hh * 2048
                mx = mxp[u % 4]
                prev = [None]

                def step(eng, fn, extra=(), cname=None):
                    w = list(extra)
                    if prev[0] is not None and prev[0][0] != cname:
                        w.append(prev[0])
                    cn = {"vector": "F", "scalar": "FS", "gpsimd": "FG",
                          "tensor": "M", "sync": "S"}[eng]
                    ser.p(eng, fn, waits=w, inc=(cn, 16 if eng == "sync" else 1))
                    prev[0] = (cn, c[cn])

                w0 = [("FS", ev_box[u * 4 + 3])]
                if u >= 1:
                    w0.append(f_ch[u - 1])
                if hh == 0:
                    step("vector", lambda e, h, mx=mx: e.memset(mx[:, 13:16], 0.0),
                         extra=w0, cname="X")
                else:
                    w0.append(("S", s_out[ct]))
                    step("sync", lambda e, h, mx=mx, ct=ct: e.dma_start(
                        out=mx[:, 13:16], in_=s_mx[ct * 128:(ct + 1) * 128, 2045:2048]),
                        extra=w0, cname="X")
                step("vector", lambda e, h, mx=mx, ct=ct: e.tensor_scalar_mul(
                    t_a[:], mx[:, 16:16 + 2048], cw_s[:, ct * 4 + 3:ct * 4 + 4]),
                    cname="F")
                step("vector", lambda e, h, mx=mx, ct=ct: e.scalar_tensor_tensor(
                    t_b[:], mx[:, 15:15 + 2048], cw_s[:, ct * 4 + 2:ct * 4 + 3],
                    t_a[:], ALU.mult, ALU.add), cname="F")
                step("vector", lambda e, h, mx=mx, ct=ct: e.scalar_tensor_tensor(
                    t_a[:], mx[:, 14:14 + 2048], cw_s[:, ct * 4 + 1:ct * 4 + 2],
                    t_b[:], ALU.mult, ALU.add), cname="F")
                step("vector", lambda e, h, mx=mx, ct=ct: e.scalar_tensor_tensor(
                    t_b[:], mx[:, 13:13 + 2048], cw_s[:, ct * 4 + 0:ct * 4 + 1],
                    t_a[:], ALU.mult, ALU.add), cname="F")
                step("scalar", lambda e, h, mx=mx: e.activation(
                    mx[:, 16:16 + 2048], t_b[:], AF.Silu), cname="X")
                step("sync", lambda e, h, mx=mx, ct=ct, t0=t0: e.dma_start(
                    out=s_mx[ct * 128:(ct + 1) * 128, t0:t0 + 2048],
                    in_=mx[:, 16:16 + 2048]),
                    cname="X")
                f_ch[u] = prev[0]
                if hh == 0:
                    s_out[ct] = c["S"]

            emit_acts(0)
            for u in range(65):
                if u == 32:
                    emit_acts(1)
                if u < 64:
                    emit_proj(u)
                if u >= 1:
                    emit_chain(u - 1)
            ser.end()

        sec_qkvconv()

        def sec_l2():
            ser.begin()
            mbuf = [WB[:, 0:4096], WB[:, 4096:8192]]
            sqb = [WB[:, 8192:12288], WB[:, 12288:16384]]
            qnb = [WB[:, 16384:16896], WB[:, 16896:17408]]
            rows_f = [WF[:, 0:4096][0:1, :], WF[:, 4096:8192][0:1, :]]
            rows_b = [WB[:, 17408:21504][0:1, :], WB[:, 21504:25600][0:1, :]]
            ld_box = {}
            sq_box = {}
            ch_last = {}
            qst = []
            for ct in range(16):
                mb = mbuf[ct % 2]
                w = [("F", ch_last[ct - 2])] if ct >= 2 else []
                ser.p("sync", lambda e, h, ct=ct, mb=mb: e.dma_start(
                    out=mb[:, :], in_=s_mx[ct * 128:(ct + 1) * 128, :]),
                    waits=w, inc=("L", 16))
                ld_box[ct] = c["L"]
                w = [("L", ld_box[ct])]
                if ct >= 2:
                    w.append(("FS", sq_box[ct - 2]))
                ser.p("scalar", lambda e, h, ct=ct, mb=mb, sq=sqb[ct % 2]:
                      e.activation(sq[:], mb[:, :], AF.Square), waits=w, inc=("FS", 1))
                sq_box[ct] = c["FS"]
                rf = rows_f[ct % 2]
                rb_ = rows_b[ct % 2]
                prevrow = 0
                for n in range(8):
                    w = [("FS", sq_box[ct])]
                    if n == 0 and ct >= 1:
                        w.append(("F", ch_last[ct - 1]))
                    if prevrow:
                        w.append(("F", prevrow))
                    ser.p("tensor", lambda e, h, n=n, sq=sqb[ct % 2]: e.matmul(
                        P_ROW[:, :], ones_c[:], sq[:, n * 512:(n + 1) * 512],
                        start=True, stop=True), waits=w, inc=("M", 1))
                    ser.p("vector", lambda e, h, n=n, rf=rf: e.tensor_copy(
                        rf[0:1, n * 512:(n + 1) * 512], P_ROW[:, :]),
                        waits=[("M", c["M"])], inc=("F", 1))
                    prevrow = c["F"]
                ser.p("scalar", lambda e, h, rf=rf: e.activation(
                    rf[:], rf[:], AF.Ln, bias=c_eps6[:]),
                    waits=[("F", prevrow)], inc=("FS", 1))
                if ct < 8:
                    ser.p("scalar", lambda e, h, rf=rf: e.activation(
                        rf[:], rf[:], AF.Exp, scale=-0.5, bias=c_lnq[:]),
                        waits=[("FS", c["FS"])], inc=("FS", 1))
                else:
                    ser.p("scalar", lambda e, h, rf=rf: e.activation(
                        rf[:], rf[:], AF.Exp, scale=-0.5),
                        waits=[("FS", c["FS"])], inc=("FS", 1))
                ser.p("vector", lambda e, h, rf=rf, rb_=rb_: e.tensor_copy(rb_[:], rf[:]),
                      waits=[("FS", c["FS"])], inc=("F", 1))
                cast_a = c["F"]
                prev_stt = ch_last.get(ct - 1, 0)
                for n in range(8):
                    w = [("F", cast_a)]
                    if prev_stt:
                        w.append(("F", prev_stt))
                    ser.p("tensor", lambda e, h, n=n, rb_=rb_: e.matmul(
                        P_B[:, :], ones_r[:], rb_[0:1, n * 512:(n + 1) * 512],
                        start=True, stop=True), waits=w, inc=("M", 1))
                    bca = c["M"]
                    w2 = [("M", bca)]
                    if len(qst) >= 2:
                        w2.append(("S", qst[-2]))
                    ser.p("vector", lambda e, h, n=n, mb=mb, qn=qnb[n % 2]:
                          e.scalar_tensor_tensor(qn[:], P_B[:, :], 1.0,
                                                 mb[:, n * 512:(n + 1) * 512],
                                                 ALU.mult, ALU.mult),
                          waits=w2, inc=("F", 1))
                    prev_stt = c["F"]
                    ser.p("sync", lambda e, h, ct=ct, n=n, qn=qnb[n % 2]: e.dma_start(
                        out=s_mx[ct * 128:(ct + 1) * 128, n * 512:(n + 1) * 512],
                        in_=qn[:]), waits=[("F", c["F"])], inc=("S", 16))
                    qst.append(c["S"])
                ch_last[ct] = c["F"]
            ser.end()

        sec_l2()

        def sec_z():
            ser.begin()
            wz = WB[:, 0:KT * 2048]
            for zmt in range(16):
                ser.p("sync", lambda e, h, zmt=zmt: e.dma_start(
                    out=wz.rearrange("p (k n) -> p k n", k=KT)[:, :, zmt * 128:(zmt + 1) * 128],
                    in_=s_wzb[zmt].rearrange("p (k m) -> p k m", m=128)),
                    waits=([("LG", PC["wzb"])] if zmt == 0 else []),
                    inc=("L", 16))
            wz_abs = c["L"]
            lts = [WB[:, 32768:34816], WB[:, 34816:36864]]
            obufs = [WB[:, 36864:37376], WB[:, 37376:37888]]
            st_box = {}
            mm_box = {}
            ev_box = {}
            for tm in range(T // 128):
                lt = lts[tm % 2]
                w = [("M", mm_box[(tm - 2) * 4 + 3])] if tm >= 2 else []
                ser.p("scalar", lambda e, h, tm=tm, lt=lt: e.dma_start(
                    out=lt.rearrange("p (k n) -> p k n", k=KT),
                    in_=s_ln3g.rearrange("(k p) t -> p k t", p=128)[:, :, tm * 128:(tm + 1) * 128]),
                    waits=w, inc=("LS", 16))
                lt_a = c["LS"]
                for zn in range(4):
                    u = tm * 4 + zn
                    bank = PB[u % 3]
                    for kt in range(KT):
                        ww = []
                        if kt == 0:
                            ww = [("L", wz_abs), ("LS", lt_a)]
                            if u >= 3:
                                ww.append(("E", ev_box[u - 3]))
                        ser.p("tensor", lambda e, h, kt=kt, lt=lt, bank=bank, zn=zn:
                              e.matmul(bank[:, :], lt[:, kt * 128:(kt + 1) * 128],
                                       wz[:, kt * 2048 + zn * 512: kt * 2048 + (zn + 1) * 512],
                                       start=(kt == 0), stop=(kt == KT - 1)),
                              waits=ww, inc=(("M", 1) if kt == KT - 1 else None))
                    mm_box[u] = c["M"]
                    ob = obufs[u % 2]
                    w = [("M", mm_box[u])]
                    if u >= 2:
                        w.append(("S", st_box[u - 2]))
                    ser.p("vector", lambda e, h, ob=ob, bank=bank: e.tensor_copy(ob[:], bank[:, :]),
                          waits=w, inc=("E", 1))
                    ev_box[u] = c["E"]
                    ser.p("sync", lambda e, h, ob=ob, tm=tm, zn=zn: e.dma_start(
                        out=s_zt[tm * 128:(tm + 1) * 128, zn * 512:(zn + 1) * 512], in_=ob[:]),
                        waits=[("E", ev_box[u])], inc=("S", 16))
                    st_box[u] = c["S"]
            ser.end()

        sec_z()

        def emit_ba():
            wb_t = WW[:, 0:KT * 16]
            wa_t = WW[:, KT * 16:KT * 32]
            rbuf = WB[:, 0:KT * 512]
            ser.d(lambda e, h: e.dma_start(
                out=wb_t.rearrange("p (k n) -> p k n", k=KT),
                in_=s_wbab.rearrange("(k p) n -> p k n", p=128)[:, :, 0:16]))
            ser.d(lambda e, h: e.dma_start(
                out=wa_t.rearrange("p (k n) -> p k n", k=KT),
                in_=s_wbab.rearrange("(k p) n -> p k n", p=128)[:, :, 16:32]))
            for n in range(8):
                ser.d(lambda e, h, n=n: e.dma_start(
                    out=rbuf.rearrange("p (k t) -> p k t", k=KT),
                    in_=s_ln3g.rearrange("(k p) t -> p k t", p=128)[:, :, n * 512:(n + 1) * 512]))
                for w_t, dstb in ((wb_t, GPb), (wa_t, GPa)):
                    for kt in range(KT):
                        ser.t(lambda e, h, kt=kt, w_t=w_t: e.matmul(
                            P_SM[0:16, :], w_t[:, kt * 16:(kt + 1) * 16],
                            rbuf[:, kt * 512:(kt + 1) * 512],
                            start=(kt == 0), stop=(kt == KT - 1)))
                    ser.v(lambda e, h, n=n, dstb=dstb: e.tensor_copy(
                        dstb[:, n * 512:(n + 1) * 512], P_SM[0:16, :]))

        ser.p("sync", lambda e, h: e.nop(), waits=[("LG", PC["wbab"])], inc=("G", 1))
        emit_ba()

        def emit_gates():
            ser.v(lambda e, h: e.tensor_scalar_add(GX[:, :], GPa[:, :], dtb_s[:]))
            ser.v(lambda e, h: e.tensor_scalar_mul(GPa[:, :], GX[:, :], -1.0))
            ser.v(lambda e, h: e.tensor_max(GS2[:, :], GX[:, :], GPa[:, :]))
            ser.s(lambda e, h: e.activation(GPa[:, :], GS2[:, :], AF.Exp, scale=-1.0))
            ser.s(lambda e, h: e.activation(GS2[:, :], GPa[:, :], AF.Ln, bias=c_one16[:]))
            ser.v(lambda e, h: e.tensor_relu(GPa[:, :], GX[:, :]))
            ser.v(lambda e, h: e.tensor_add(GX[:, :], GS2[:, :], GPa[:, :]))
            ser.v(lambda e, h: e.tensor_scalar_mul(GX[:, :], GX[:, :], negea[:]))
            ser.s(lambda e, h: e.activation(GPb[:, :], GPb[:, :], AF.Sigmoid))
            for cc_ in range(NCH):
                ser.t(lambda e, h, cc_=cc_: e.transpose(
                    PB[2][:, 0:16], GX[:, cc_ * 128:(cc_ + 1) * 128], ident_f[0:16, 0:16]))
                ser.v(lambda e, h, cc_=cc_: e.tensor_copy(gtm[:, cc_ * 16:(cc_ + 1) * 16],
                                                          PB[2][:, 0:16]))
                ser.t(lambda e, h, cc_=cc_: e.transpose(
                    PB[2][:, 0:16], GPb[:, cc_ * 128:(cc_ + 1) * 128], ident_f[0:16, 0:16]))
                ser.v(lambda e, h, cc_=cc_: e.tensor_copy(btm[:, cc_ * 16:(cc_ + 1) * 16],
                                                          PB[2][:, 0:16]))
            ser.t(lambda e, h: e.matmul(PB[2][:, :], triu_f[:], gtm[:], start=True, stop=True))
            ser.v(lambda e, h: e.tensor_copy(Gtm[:], PB[2][:, :]))
            ser.s(lambda e, h: e.activation(Btm[:], Gtm[:], AF.Exp))
            ser.v(lambda e, h: e.tensor_scalar_mul(negBtm[:], Btm[:], -1.0))
            ser.s(lambda e, h: e.activation(lnbtm[:], btm[:], AF.Ln))
            ser.v(lambda e, h: e.tensor_add(GLBtm[:], Gtm[:], lnbtm[:]))
            ser.d(lambda e, h: e.dma_start(out=glbrow4[:, 0:512],
                                           in_=Gtm[127:128, :]))
            ser.t(lambda e, h: e.matmul(PB[2][:, :], ones_rf[:], glbrow4[:, 0:512],
                                        start=True, stop=True))
            ser.v(lambda e, h: e.tensor_copy(eNGt[:], PB[2][:, :]))
            ser.s(lambda e, h: e.activation(BLtile[:], eNGt[:], AF.Exp))
            ser.v(lambda e, h: e.tensor_sub(eNGt[:], eNGt[:], Gtm[:]))
            ser.s(lambda e, h: e.activation(scolTm[:], eNGt[:], AF.Exp))

        emit_gates()

        S_f = [WF[:, 9248:11296], WF[:, 11296:13344]]
        BLtile = WF[:, 13344:13856]
        scolTm = WF[:, 13856:14368]
        eNGt = WF[:, 14368:14880]
        S_b = WB[:, 37888:39936]
        grow4 = sb("grow4", [1, 768], F32)
        negrow4 = sb("negrow4", [1, 768], F32)
        glbrow4 = sb("glbrow4", [1, 768], F32)
        scol4 = sb("scol4", [128, 6], F32)
        colGL4 = sb("colGL4", [128, 6], F32)
        colBL4 = sb("colBL4", [128, 6], F32)

        def sec_delta():
            ser.begin()
            qh2 = [WB[:, 0:1024], WB[:, 1024:2048]]
            kh2 = [WB[:, 2048:3072], WB[:, 3072:4096]]
            kt2 = [WB[:, 4096:5120], WB[:, 5120:6144]]
            vt2 = [WB[:, 6144:8192], WB[:, 8192:10240]]
            zt2 = [WB[:, 10240:12288], WB[:, 12288:14336]]
            slz2 = [WB[:, 14336:16384], WB[:, 16384:18432]]
            NSLOT = 6
            SLOT = []
            for sl in range(NSLOT):
                b0 = 18432 + sl * 2560
                SLOT.append(dict(
                    Mb=WB[:, b0:b0 + 128], Nb=WB[:, b0 + 128:b0 + 256],
                    Wtb=WB[:, b0 + 256:b0 + 384], Ua=WB[:, b0 + 384:b0 + 512],
                    Ub=WB[:, b0 + 512:b0 + 640],
                    Pow=[WB[:, b0 + 640 + i * 128: b0 + 768 + i * 128] for i in range(12)],
                    PowPair=[WB[:, b0 + 640 + i * 256: b0 + 896 + i * 256] for i in range(5)],
                    Kp=WB[:, b0 + 2176:b0 + 2304],
                    eDs=WF[:, sl * 128:(sl + 1) * 128],
                    eDt=WF[:, 768 + sl * 128: 768 + (sl + 1) * 128],
                    eDsT=WF[:, 1536 + sl * 128: 1536 + (sl + 1) * 128],
                    o1=WF[:, 2304 + sl * 128: 2304 + (sl + 1) * 128],
                    grow=grow4[:, sl * 128:(sl + 1) * 128],
                    negrow=negrow4[:, sl * 128:(sl + 1) * 128],
                    glbrow=glbrow4[:, sl * 128:(sl + 1) * 128],
                    scol=scol4[:, sl:sl + 1], colGL=colGL4[:, sl:sl + 1],
                    colBL=colBL4[:, sl:sl + 1],
                    p_row=PB[sl][0:1, 384:512],
                    p_bc=PB[sl][:, 448:449],
                    p_d=PB[sl][:, 128:256],
                    p_a=PB[sl][:, 0:128],
                    p_q=PB[sl][:, 256:384],
                    p_w=PB[sl][:, 0:128],
                    p_s=PB[sl][:, 0:128],
                    last=0,
                ))
            cnb2 = [WB[:, 33792:35840], WB[:, 35840:37888]]
            O_f2 = [WF[:, 3072:5120], WF[:, 5120:7168]]
            sq_f = WF[:, 7168:9216]
            rstd2 = [WF[:, 9216:9232], WF[:, 9232:9248]]
            ser.p("vector", lambda e, h: e.memset(S_f[0], 0.0), inc=("F", 1))
            ser.p("vector", lambda e, h: e.memset(S_b, 0.0), inc=("F", 1))
            init_f = ("F", c["F"])
            for sl in range(NSLOT):
                SLOT[sl]["last"] = init_f
            gate_done = {}
            chunk_last = {}

            def emit_unit(cci, vh, l_abs, prep_f, sl):
                S = SLOT[sl]
                cb = cci % 2
                cur, nxt = S_f[cci % 2], S_f[(cci + 1) % 2]
                kh = vh // 2
                cv = cci * 16 + vh
                qh = qh2[cb][:, kh * 128:(kh + 1) * 128]
                khh = kh2[cb][:, kh * 128:(kh + 1) * 128]
                Gc = Gtm[:, cv:cv + 1]
                Bc = Btm[:, cv:cv + 1]
                nBc = negBtm[:, cv:cv + 1]
                ops = []

                def T(fn, w=()):
                    ops.append(("tensor", fn, ("M", 1), w))

                def V(fn, w=()):
                    ops.append(("vector", fn, ("F", 1), w))

                def SS(fn, w=()):
                    ops.append(("scalar", fn, ("FS", 1), w))

                def G_(fn, w=()):
                    ops.append(("gpsimd", fn, ("FG", 1), w))

                T(lambda e, h: e.transpose(S["p_row"], Gc, ident_f[:]))
                SS(lambda e, h: e.copy(S["grow"], S["p_row"]))
                V(lambda e, h: e.tensor_scalar_mul(S["negrow"], S["grow"], -1.0))
                T(lambda e, h: e.matmul(S["p_d"], S["grow"], ones_rf[:],
                                        start=True, stop=False))
                T(lambda e, h: e.matmul(S["p_d"], ones_rf[:], S["negrow"],
                                        start=False, stop=True))
                SS(lambda e, h: e.activation(S["eDs"], S["p_d"], AF.Exp,
                                             bias=lnbtm[:, cv:cv + 1]))
                G_(lambda e, h: e.affine_select(
                    out=S["eDs"], in_=S["eDs"], pattern=[[-1, 128]], compare_op=ALU.is_ge,
                    fill=0.0, base=-1, channel_multiplier=1))
                SS(lambda e, h: e.activation(S["eDt"], S["p_d"], AF.Exp, scale=-1.0))
                G_(lambda e, h: e.affine_select(
                    out=S["eDt"], in_=S["eDt"], pattern=[[1, 128]], compare_op=ALU.is_ge,
                    fill=0.0, base=0, channel_multiplier=-1))
                T(lambda e, h: e.matmul(S["p_a"], khh, khh, start=True, stop=True),
                  w=[("L", l_abs)])
                T(lambda e, h: e.matmul(S["p_q"], khh, qh, start=True, stop=True))
                V(lambda e, h: e.tensor_tensor(S["Mb"], S["p_a"], S["eDs"], ALU.mult))
                V(lambda e, h: e.tensor_tensor(S["Wtb"], S["p_q"], S["eDt"], ALU.mult))
                T(lambda e, h: e.transpose(S["p_d"], S["eDs"], ident_f[:]))
                SS(lambda e, h: e.copy(S["eDsT"], S["p_d"]))
                V(lambda e, h: e.tensor_tensor(S["Nb"], S["p_a"], S["eDsT"], ALU.mult))
                Pc, Nc = S["Mb"], S["Nb"]
                pw2 = PB[sl][:, 0:256]
                for lvl in range(5):
                    Pn, Nn = S["Pow"][lvl * 2], S["Pow"][lvl * 2 + 1]
                    Pp = S["PowPair"][lvl]
                    T(lambda e, h, Nc=Nc, Pc=Pc: e.matmul(PB[sl][:, 0:128], Nc, Pc,
                                                          start=True, stop=True))
                    T(lambda e, h, Nc=Nc, Pc=Pc: e.matmul(PB[sl][:, 128:256], Pc, Nc,
                                                          start=True, stop=True))
                    if lvl % 2 == 0:
                        SS(lambda e, h, Pp=Pp: e.copy(Pp, pw2))
                    else:
                        V(lambda e, h, Pp=Pp: e.tensor_copy(Pp, pw2))
                    Pc, Nc = Pn, Nn
                N64 = S["Pow"][10]
                T(lambda e, h, Pc=Pc, Nc=Nc: e.matmul(PB[sl][:, 0:128], Pc, Nc,
                                                      start=True, stop=True))
                SS(lambda e, h, N64=N64: e.copy(N64[:], PB[sl][:, 0:128]))
                T(lambda e, h: e.matmul(S["p_s"], khh, S_b[:, vh * 128:(vh + 1) * 128],
                                        start=True, stop=True))
                V(lambda e, h: e.scalar_tensor_tensor(
                    S["o1"], S["p_s"], nBc, vt2[cb][:, vh * 128:(vh + 1) * 128],
                    ALU.mult, ALU.add))
                V(lambda e, h: e.tensor_scalar_mul(S["Ua"], S["o1"], btm[:, cv:cv + 1]))
                Ucur, Uoth = S["Ua"], S["Ub"]
                for Nl in [S["Pow"][10], S["Pow"][9], S["Pow"][7], S["Pow"][5],
                           S["Pow"][3], S["Pow"][1]]:
                    T(lambda e, h, Nl=Nl, Uc=Ucur: e.matmul(S["p_w"], Nl, Uc,
                                                            start=True, stop=True))
                    V(lambda e, h, Uc=Ucur, Uo=Uoth: e.tensor_add(Uo[:], S["p_w"], Uc[:]))
                    Ucur, Uoth = Uoth, Ucur
                T(lambda e, h, Uc=Ucur: e.matmul(S["p_w"], S["Nb"], Uc,
                                                 start=True, stop=True))
                V(lambda e, h, Uc=Ucur, Uo=Uoth: e.tensor_sub(Uo[:], Uc[:], S["p_w"]))
                Ucur, Uoth = Uoth, Ucur
                T(lambda e, h: e.matmul(S["p_s"], qh, S_b[:, vh * 128:(vh + 1) * 128],
                                        start=True, stop=True))
                V(lambda e, h: e.tensor_scalar_mul(S["o1"], S["p_s"], Bc))
                T(lambda e, h, Uc=Ucur: e.matmul(S["p_w"], S["Wtb"], Uc,
                                                 start=True, stop=True))
                V(lambda e, h, cci=cci: e.tensor_add(
                    O_f2[cci % 2][:, vh * 128:(vh + 1) * 128], S["p_w"], S["o1"]))
                o_write = True
                V(lambda e, h: e.tensor_scalar_mul(
                    S["Kp"], kt2[cb][:, kh * 128:(kh + 1) * 128],
                    scolTm[:, cv:cv + 1]))
                T(lambda e, h, Uc=Ucur: e.matmul(S["p_w"], S["Kp"], Uc,
                                                 start=True, stop=True))
                V(lambda e, h: e.scalar_tensor_tensor(
                    nxt[:, vh * 128:(vh + 1) * 128], cur[:, vh * 128:(vh + 1) * 128],
                    BLtile[:, cv:cv + 1], S["p_w"], ALU.mult, ALU.add))
                SS(lambda e, h: e.copy(S_b[:, vh * 128:(vh + 1) * 128],
                                       nxt[:, vh * 128:(vh + 1) * 128]))
                return ops

            def emit_group(cci, vhs, l_abs, prep_f):
                chains = []
                for i, vh in enumerate(vhs):
                    ops = emit_unit(cci, vh, l_abs, prep_f, i)
                    chains.append([ops, SLOT[i]["last"]])
                maxlen = max(len(ch[0]) for ch in chains)
                for k in range(maxlen):
                    for i in range(len(vhs)):
                        ops, prev = chains[i]
                        if k >= len(ops):
                            continue
                        eng, fn, inc, extra = ops[k]
                        w = list(extra)
                        if prev is not None:
                            w.append(prev)
                        ser.p(eng, fn, waits=w, inc=inc)
                        chains[i][1] = (inc[0], c[inc[0]])
                for i in range(len(vhs)):
                    SLOT[i]["last"] = chains[i][1]
                return [chains[i][1] for i in range(len(vhs))]

            for cci in range(NCH):
                t0 = cci * 128
                cb = cci % 2
                wl = [chunk_last[cci - 2]] if cci >= 2 else []
                ser.p("sync", lambda e, h, t0=t0, cb=cb: e.dma_start(
                    out=qh2[cb].rearrange("p (k t) -> p k t", k=NKH),
                    in_=s_mx.rearrange("(k p) t -> p k t", p=128)[:, 0:NKH, t0:t0 + 128]),
                    waits=wl, inc=("L", 16))
                ser.p("sync", lambda e, h, t0=t0, cb=cb: e.dma_start(
                    out=kh2[cb].rearrange("p (k t) -> p k t", k=NKH),
                    in_=s_mx.rearrange("(k p) t -> p k t", p=128)[:, NKH:2 * NKH, t0:t0 + 128]),
                    inc=("L", 16))
                ser.p("sync", lambda e, h, t0=t0, cb=cb: e.dma_start_transpose(
                    out=kt2[cb][:, :], in_=s_mx[1024:2048, t0:t0 + 128]), inc=("L", 16))
                ser.p("sync", lambda e, h, t0=t0, cb=cb: e.dma_start_transpose(
                    out=vt2[cb][:, :], in_=s_mx[2048:4096, t0:t0 + 128]), inc=("L", 16))
                ser.p("sync", lambda e, h, t0=t0, cb=cb: e.dma_start(
                    out=zt2[cb][:, :], in_=s_zt[t0:t0 + 128, :]), inc=("L", 16))
                l_abs = c["L"]
                ser.p("scalar", lambda e, h, cb=cb: e.activation(
                    slz2[cb][:], zt2[cb][:], AF.Silu),
                    waits=[("L", l_abs)] + wl, inc=("FS", 1))
                ser.p("vector", lambda e, h, cb=cb: e.tensor_tensor(
                    slz2[cb].rearrange("p (r n) -> p r n", r=NVH),
                    slz2[cb].rearrange("p (r n) -> p r n", r=NVH),
                    NWbc[:, :].unsqueeze(1).broadcast_to([128, NVH, 128]), ALU.mult),
                    waits=[("FS", c["FS"])], inc=("F", 1))
                prep_f = ("F", c["F"])
                for vhs in ([0, 1, 2, 3, 4, 5], [6, 7, 8, 9, 10, 11],
                            [12, 13, 14, 15]):
                    emit_group(cci, vhs, l_abs, prep_f)
                lastfs = [SLOT[i]["last"] for i in range(NSLOT)]
                assert all(ff[0] == lastfs[0][0] for ff in lastfs)
                gw = [(lastfs[0][0], max(ff[1] for ff in lastfs)), prep_f]
                if cci >= 2:
                    gw.append(gate_done[cci - 2])
                O_f = O_f2[cci % 2]
                rstd_f = rstd2[cci % 2]
                cn_b = cnb2[cci % 2]
                ser.p("vector", lambda e, h, O_f=O_f: e.tensor_tensor(
                    sq_f[:], O_f[:], O_f[:], ALU.mult), waits=gw, inc=("F", 1))
                ser.p("vector", lambda e, h, rstd_f=rstd_f: e.reduce_sum(
                    rstd_f[:], sq_f.rearrange("p (a b) -> p a b", a=NVH), axis=AX.X),
                    waits=[("F", c["F"])], inc=("F", 1))
                ser.p("scalar", lambda e, h, rstd_f=rstd_f: e.activation(
                    rstd_f[:], rstd_f[:], AF.Ln, bias=c_eps6c[:], scale=1.0 / 128.0),
                    waits=[("F", c["F"])], inc=("FS", 1))
                ser.p("scalar", lambda e, h, rstd_f=rstd_f: e.activation(
                    rstd_f[:], rstd_f[:], AF.Exp, scale=-0.5),
                    waits=[("FS", c["FS"])], inc=("FS", 1))
                rf = c["FS"]
                for vh in range(NVH):
                    ser.p("vector", lambda e, h, vh=vh, O_f=O_f, rstd_f=rstd_f,
                          cn_b=cn_b, cb=cb: e.scalar_tensor_tensor(
                              cn_b[:, vh * 128:(vh + 1) * 128],
                              O_f[:, vh * 128:(vh + 1) * 128],
                              rstd_f[:, vh:vh + 1],
                              slz2[cb][:, vh * 128:(vh + 1) * 128],
                              ALU.mult, ALU.mult),
                          waits=([("FS", rf)] if vh == 0 else []), inc=("F", 1))
                gate_done[cci] = ("F", c["F"])
                chunk_last[cci] = ("F", c["F"])
                ser.p("sync", lambda e, h, t0=t0, cn_b=cn_b: e.dma_start(
                    out=cc3_in[t0:t0 + 128, :], in_=cn_b[:, :]),
                    waits=[gate_done[cci]], inc=("S", 16))
            ser.end()

        sec_delta()

        ser.cc(lambda e, h: e.collective_compute(
            "AllGather", ALU.bypass,
            replica_groups=[[0, 1], [2, 3], [4, 5], [6, 7]],
            ins=[cc3_in.opt()], outs=[cc3_out.opt()]))

        for r in range(2):
            ser.d(lambda e, h, r=r: e.dma_start(
                out=s_cnw[:, r * 2048:(r + 1) * 2048],
                in_=cc3_out[bass.DynSlice(h["par"] * TOWN + r * T, TOWN), :]))

        def sec_cnt():
            ser.begin()
            tbs = [WB[:, 0:2048], WB[:, 2048:4096]]
            l_box = {}
            s_box = {}
            for vc in range(32):
                tb = tbs[vc % 2]
                w = [("S", s_box[vc - 2])] if vc >= 2 else []
                ser.p("sync", lambda e, h, vc=vc, tb=tb: e.dma_start_transpose(
                    out=tb[:, :], in_=s_cnw[:, vc * 128:(vc + 1) * 128]),
                    waits=w, inc=("L", 16))
                l_box[vc] = c["L"]
                ser.p("sync", lambda e, h, vc=vc, tb=tb: e.dma_start(
                    out=s_cnf[vc * 128:(vc + 1) * 128, :], in_=tb[:, :]),
                    waits=[("L", l_box[vc])], inc=("S", 16))
                s_box[vc] = c["S"]
            ser.end()

        sec_cnt()
        sec_nproj(32, [(s_wo, 4096)], s_cnf, resid2, resid3, lgw=PC["wo"])

        emit_ln(resid3, s_ln4, TOWN)
        sec_ffn_hidden(s_ln4)
        sec_nproj(HFF // 128, [(qg3v[0:KT], 2816), (qg3v[KT:2 * KT], 2816)],
                  s_hid, resid3, yT)

        holder = {}

        def body_sync(eng):
            holder["par"] = eng.bass.partition_id([SP]) % 2
            ser.emit_for("sync", eng, holder, semh)

        block.sync(body_sync)
        def body_gpsimd(eng):
            holder["par_g"] = eng.bass.partition_id([mybir.EngineType.Pool]) % 2
            ser.emit_for("gpsimd", eng, holder, semh)

        block.gpsimd(body_gpsimd)
        block.vector(lambda eng: ser.emit_for("vector", eng, holder, semh))

        def body_scalar(eng):
            holder["par_s"] = eng.bass.partition_id([mybir.EngineType.Activation]) % 2
            ser.emit_for("scalar", eng, holder, semh)

        block.scalar(body_scalar)
        block.tensor(lambda eng: ser.emit_for("tensor", eng, holder, semh))

    return nc


_NC_CACHE = None


def kernel(L0_attn_a_x, L0_attn_a_W_attn, L0_attn_a_W_proj, L0_attn_b_Wqkv,
           L0_attn_b_Wz, L0_attn_b_Wb, L0_attn_b_Wa, L0_attn_b_conv_w,
           L0_attn_b_dt_bias, L0_attn_b_A_log, L0_attn_b_norm_w, L0_attn_b_Wout,
           L0_ffn_w1, L0_ffn_w2, L0_ffn_w3):
    global _NC_CACHE
    if _NC_CACHE is None:
        _NC_CACHE = build_nc()
    nc = _NC_CACHE

    x = np.asarray(L0_attn_a_x, np.float32)
    Wa_ = np.asarray(L0_attn_a_W_attn, np.float32)
    Wp = np.asarray(L0_attn_a_W_proj, np.float32)
    Wqkv = np.asarray(L0_attn_b_Wqkv, np.float32)
    Wz = np.asarray(L0_attn_b_Wz, np.float32)
    Wb_ = np.asarray(L0_attn_b_Wb, np.float32)
    Waa = np.asarray(L0_attn_b_Wa, np.float32)
    cw = np.asarray(L0_attn_b_conv_w, np.float32)
    dtbv = np.asarray(L0_attn_b_dt_bias, np.float32)
    alogv = np.asarray(L0_attn_b_A_log, np.float32)
    nwv = np.asarray(L0_attn_b_norm_w, np.float32)
    Wo = np.asarray(L0_attn_b_Wout, np.float32)
    w1 = np.asarray(L0_ffn_w1, np.float32)
    w2 = np.asarray(L0_ffn_w2, np.float32)
    w3 = np.asarray(L0_ffn_w3, np.float32)

    ct = lambda a: np.ascontiguousarray(a)
    in_maps = []
    for cc_ in range(8):
        b, s = cc_ // 2, cc_ % 2
        qs, ks, vs = 1024 * s, 2048 + 1024 * s, 4096 + 1024 * s
        in_maps.append({
            "xT": ct(x[b].T),
            "waT": ct(np.concatenate([Wa_[qs:qs + 1024], Wa_[ks:ks + 1024],
                                      Wa_[vs:vs + 1024]], 0).T),
            "wpT": ct(Wp.T),
            "wqkvT": ct(np.concatenate([Wqkv[1024 * s:1024 * s + 1024],
                                        Wqkv[2048 + 1024 * s:2048 + 1024 * s + 1024],
                                        Wqkv[4096 + 2048 * s:4096 + 2048 * s + 2048]], 0).T),
            "wzT": ct(Wz[2048 * s:2048 * s + 2048].T),
            "wbaT": ct(np.concatenate([Wb_[16 * s:16 * s + 16],
                                       Waa[16 * s:16 * s + 16]], 0).T),
            "cwm": ct(np.concatenate([cw[1024 * s:1024 * s + 1024],
                                      cw[2048 + 1024 * s:2048 + 1024 * s + 1024],
                                      cw[4096 + 2048 * s:4096 + 2048 * s + 2048]], 0)),
            "dtb": ct(dtbv[16 * s:16 * s + 16].reshape(16, 1)),
            "alog": ct(alogv[16 * s:16 * s + 16].reshape(16, 1)),
            "nw": ct(nwv.reshape(1, 128)),
            "w1T": ct(w1.T), "w2T": ct(w2.T), "w3T": ct(w3.T),
            "woT": ct(Wo.T),
        })

    res = run_bass_kernel_spmd(nc, in_maps, core_ids=list(range(8)))
    out = np.zeros((B, T, C), np.float32)
    for cc_ in range(8):
        b, s = cc_ // 2, cc_ % 2
        out[b, 2048 * s:2048 * s + 2048, :] = res.results[cc_]["yT"].T
    return out


# version 33 (final)
# speedup vs baseline: 1.0186x; 1.0186x over previous
import numpy as np
import concourse.bass as bass
import concourse.mybir as mybir
from concourse.bass_utils import run_bass_kernel_spmd

F32 = mybir.dt.float32
BF16 = mybir.dt.bfloat16
I32 = mybir.dt.int32
AF = mybir.ActivationFunctionType
ALU = mybir.AluOpType
AX = mybir.AxisListType
SP = mybir.EngineType.SP

B, T, C = 4, 4096, 2048
TOWN = 2048
NHA = 8
NKH, NVH, DK, DV = 8, 16, 128, 128
HFF = 5632
KT = C // 128
NCH = T // 128
INV_SQRT_D = 1.0 / np.sqrt(128.0)
LOG_INV_SQRT_D = float(np.log(INV_SQRT_D))

SEMS = ("G", "L", "M", "E", "F", "S", "FS", "FG", "LS", "LG")


class Ser:
    def __init__(self):
        self.steps = []
        self.c = {k: 0 for k in SEMS}
        self.sec_g = None
        self.sec_started = None

    def p(self, eng, fn, waits=(), inc=None):
        waits = list(waits)
        if self.sec_g is not None and eng not in self.sec_started:
            waits = [("G", self.sec_g)] + waits
            self.sec_started.add(eng)
        self.steps.append((eng, fn, [w for w in waits if w[1] > 0], inc))
        if inc is not None:
            self.c[inc[0]] += inc[1]

    def _ser(self, eng, fn, amt):
        assert self.sec_g is None
        self.steps.append((eng, fn, [("G", self.c["G"])] if self.c["G"] else [],
                           ("G", amt)))
        self.c["G"] += amt

    def t(self, fn):
        self._ser("tensor", fn, 1)

    def v(self, fn):
        self._ser("vector", fn, 1)

    def s(self, fn):
        self._ser("scalar", fn, 1)

    def g(self, fn):
        self._ser("gpsimd", fn, 1)

    def d(self, fn):
        self._ser("sync", fn, 16)

    def gd(self, fn):
        self._ser("gpsimd", fn, 16)

    def cc(self, fn):
        self._ser("gpsimd", fn, 1)

    def begin(self):
        assert self.sec_g is None
        self.sec_g = self.c["G"]
        self.sec_started = set()

    def end(self):
        assert self.sec_g is not None
        waits = [(k, self.c[k]) for k in ("L", "M", "E", "F", "S", "FS", "FG", "LS", "LG") if self.c[k] > 0]
        self.steps.append(("sync", lambda e, h: e.nop(), waits, ("G", 1)))
        self.c["G"] += 1
        self.sec_g = None

    def emit_for(self, name, eng, holder, sems):
        for engine, fn, waits, inc in self.steps:
            if engine != name:
                continue
            for nm, val in waits:
                eng.wait_ge(sems[nm], val)
            ins = fn(eng, holder)
            if inc is not None:
                ins.then_inc(sems[inc[0]], inc[1])
        eng.wait_ge(sems["G"], self.c["G"])


def build_nc():
    nc = bass.Bass("TRN2", target_bir_lowering=False)
    dt_in = lambda n, s: nc.dram_tensor(n, s, F32, kind="ExternalInput").ap()

    xT = dt_in("xT", [C, T])
    waT = dt_in("waT", [C, 3072])
    wpT = dt_in("wpT", [C, C])
    wqkvT = dt_in("wqkvT", [C, 4096])
    wzT = dt_in("wzT", [C, 2048])
    wbaT = dt_in("wbaT", [C, 32])
    cwm = dt_in("cwm", [4096, 4])
    dtb = dt_in("dtb", [16, 1])
    alog = dt_in("alog", [16, 1])
    nw = dt_in("nw", [1, 128])
    w1T = dt_in("w1T", [C, HFF])
    w2T = dt_in("w2T", [C, HFF])
    w3T = dt_in("w3T", [HFF, C])
    woT = dt_in("woT", [4096, C])

    yT = nc.dram_tensor("yT", [C, TOWN], F32, kind="ExternalOutput").ap()

    ds = lambda n, s, d=BF16: nc.dram_tensor(n, s, d, kind="Internal").ap()
    s_ln1 = ds("s_ln1", [C, T])
    s_q = ds("s_q", [1024, T])
    s_k = ds("s_k", [1024, T])
    s_vt = ds("s_vt", [T, 1024])
    cc1_in = ds("cc1_in", [1024, T])
    cc1_out = ds("cc1_out", [2048, T])
    s_ao = ds("s_ao", [2048, TOWN])
    s_xo = ds("s_xo", [C, TOWN], F32)
    resid1 = ds("resid1", [C, TOWN], F32)
    resid2 = ds("resid2", [C, TOWN], F32)
    resid3 = ds("resid3", [C, TOWN], F32)
    qh1 = ds("qh1", [HFF // 128, 128 * 1024])
    qh2d = ds("qh2d", [HFF // 128, 128 * 1024])
    qh3 = ds("qh3", [KT, 128 * 2816])
    qg1 = ds("qg1", [2 * (HFF // 128), 128 * 1024])
    qg2 = ds("qg2", [2 * (HFF // 128), 128 * 1024])
    qg3 = ds("qg3", [2 * KT, 128 * 2816])
    s_w1h = ds("s_w1h", [1024, HFF], F32)
    s_w2h = ds("s_w2h", [1024, HFF], F32)
    s_w3h = ds("s_w3h", [2816, C], F32)
    cc4_in = ds("cc4_in", [128, 4], F32)
    cc4_out = ds("cc4_out", [256, 4], F32)
    s_gq3 = ds("s_gq3", [4, 1], F32)
    s_hid = ds("s_hid", [HFF, TOWN])
    s_ln2 = ds("s_ln2", [C, TOWN])
    s_ln4 = ds("s_ln4", [C, TOWN])
    cc2_in = ds("cc2_in", [C, TOWN])
    cc2_out = ds("cc2_out", [2 * C, TOWN])
    s_ln3g = ds("s_ln3g", [C, T])
    s_mx = ds("s_mx", [4096, T])
    s_zt = ds("s_zt", [T, 2048])
    cc3_in = ds("cc3_in", [T, 2048])
    cc3_out = ds("cc3_out", [2 * T, 2048])
    s_cnw = ds("s_cnw", [TOWN, 4096])
    s_cnf = ds("s_cnf", [4096, TOWN])
    s_wo = ds("s_wo", [16, 128, 4096])
    s_wab = ds("s_wab", [24, 128, 2048])
    s_wpb = ds("s_wpb", [16, 128, 2048])
    s_wqb = ds("s_wqb", [32, 128, 2048])
    s_wzb = ds("s_wzb", [16, 128, 2048])
    s_wbab = ds("s_wbab", [C, 32])
    s_gq = ds("s_gq", [128, 1], F32)

    from contextlib import ExitStack
    with ExitStack() as ctx:
        sb = lambda n, s, d: ctx.enter_context(nc.sbuf_tensor(n, s, d))
        psd = lambda n, s, d: ctx.enter_context(nc.psum_tensor(n, s, d))

        ident_b = sb("ident_b", [128, 128], BF16)
        ident_f = sb("ident_f", [128, 128], F32)
        triu_f = sb("triu_f", [128, 128], F32)
        ones_c = sb("ones_c", [128, 1], BF16)
        ones_r = sb("ones_r", [1, 128], BF16)
        ones_rf = sb("ones_rf", [1, 128], F32)
        ones_cf = sb("ones_cf", [128, 1], F32)
        nw_s = sb("nw_s", [1, 128], F32)
        nw_b = sb("nw_b", [1, 128], BF16)
        NWbc = sb("NWbc", [128, 128], BF16)
        cw_s = sb("cw_s", [128, 128], F32)
        dtb_s = sb("dtb_s", [16, 1], F32)
        alog_s = sb("alog_s", [16, 1], F32)
        negea = sb("negea", [16, 1], F32)
        grow_f = sb("grow_f", [1, 128], F32)
        negrow_f = sb("negrow_f", [1, 128], F32)
        glbrow_f = sb("glbrow_f", [1, 128], F32)
        rowbuf = sb("rowbuf", [1, 512], F32)
        rowbuf2 = sb("rowbuf2", [1, 512], F32)
        rowbuf3 = sb("rowbuf3", [1, 512], F32)
        rowbf = sb("rowbf", [1, 512], BF16)
        rowbfB = sb("rowbfB", [1, 512], BF16)
        scol = sb("scol", [128, 1], F32)
        colGL = sb("colGL", [128, 1], F32)
        colBL = sb("colBL", [128, 1], F32)
        sinv_c = sb("sinv_c", [128, 1], F32)
        acc_c = sb("acc_c", [128, 1], F32)
        acc_c2 = sb("acc_c2", [128, 1], F32)
        gam_r = sb("gam_r", [1, 1], F32)
        c_eps5 = sb("c_eps5", [1, 1], F32)
        c_eps6 = sb("c_eps6", [1, 1], F32)
        c_eps6c = sb("c_eps6c", [128, 1], F32)
        c_one16 = sb("c_one16", [16, 1], F32)
        c_lnq = sb("c_lnq", [1, 1], F32)

        WB = sb("WB", [128, 39936], BF16)
        WF = sb("WF", [128, 19968], F32)
        WW = sb("WW", [128, 11264], BF16)

        GPb = WF[0:16, 0:4096]
        GPa = WF[0:16, 4096:8192]
        GX = WF[0:16, 8192:12288]
        GS2 = WF[0:16, 12288:16384]
        btm = WF[:, 16384:16896]
        gtm = WF[:, 16896:17408]
        Gtm = WF[:, 17408:17920]
        Btm = WF[:, 17920:18432]
        negBtm = WF[:, 18432:18944]
        lnbtm = WF[:, 18944:19456]
        GLBtm = WF[:, 19456:19968]

        PB = [psd(f"PB{i}", [128, 512], F32) for i in range(6)]
        P_ROW = psd("P_ROW", [1, 512], F32)
        P_B = psd("P_B", [128, 512], F32)
        P_SM = PB[4]

        semh = {k: ctx.enter_context(nc.semaphore(f"sem_{k}")) for k in SEMS}
        block = ctx.enter_context(nc.Block())
        ser = Ser()
        c = ser.c

        ser.d(lambda e, h: e.dma_start(out=nw_s[:], in_=nw))
        ser.d(lambda e, h: e.dma_start(out=dtb_s[:], in_=dtb))
        ser.d(lambda e, h: e.dma_start(out=alog_s[:], in_=alog))
        ser.d(lambda e, h: e.dma_start(
            out=cw_s[:, :].rearrange("p (t j) -> p t j", t=32),
            in_=cwm.rearrange("(t p) j -> p t j", p=128)))
        for tile, lower, upper in ((ident_b, True, True), (ident_f, True, True),
                                   (triu_f, False, True)):
            ser.v(lambda e, h, t=tile: e.memset(t[:], 1.0))
            if lower:
                ser.g(lambda e, h, t=tile: e.affine_select(
                    out=t[:], in_=t[:], pattern=[[-1, 128]], compare_op=ALU.is_ge,
                    fill=0.0, base=0, channel_multiplier=1))
            if upper:
                ser.g(lambda e, h, t=tile: e.affine_select(
                    out=t[:], in_=t[:], pattern=[[1, 128]], compare_op=ALU.is_ge,
                    fill=0.0, base=0, channel_multiplier=-1))
        for tile, val in ((ones_c, 1.0), (ones_r, 1.0), (ones_rf, 1.0),
                          (ones_cf, 1.0), (c_eps5, 1e-5), (c_eps6, 1e-6),
                          (c_eps6c, 1e-6), (c_one16, 1.0), (c_lnq, LOG_INV_SQRT_D)):
            ser.v(lambda e, h, t=tile, v=val: e.memset(t[:], v))
        ser.v(lambda e, h: e.tensor_copy(nw_b[:], nw_s[:]))
        ser.t(lambda e, h: e.matmul(P_B[:, 0:128], ones_r[:], nw_b[:], start=True, stop=True))
        ser.v(lambda e, h: e.tensor_copy(NWbc[:], P_B[:, 0:128]))
        ser.s(lambda e, h: e.activation(negea[:], alog_s[:], AF.Exp))
        ser.v(lambda e, h: e.tensor_scalar_mul(negea[:], negea[:], -1.0))
        PC = {}
        for di, (nm, dst, src, nmt) in enumerate(
                (("wab", s_wab, waT, 24), ("wpb", s_wpb, wpT, 16),
                 ("wqb", s_wqb, wqkvT, 32), ("wzb", s_wzb, wzT, 16),
                 ("wo", s_wo, woT, 16))):
            for mt in range(nmt):
                w0 = [("G", c["G"])] if (di == 0 and mt == 0 and c["G"]) else []
                ser.p("gpsimd", lambda e, h, d=dst, s=src, mt=mt: e.dma_start(
                    out=d[mt].rearrange("p (k m) -> p k m", m=128),
                    in_=s.rearrange("(k p) n -> p k n", p=128)[:, :, mt * 128:(mt + 1) * 128]),
                    waits=w0, inc=("LG", 16))
            PC[nm] = c["LG"]
        ser.p("gpsimd", lambda e, h: e.dma_start(out=s_wbab, in_=wbaT),
              inc=("LG", 16))
        PC["wbab"] = c["LG"]
        for (full, half) in ((w1T, s_w1h), (w2T, s_w2h), (w3T, s_w3h)):
            rows = half.shape[0]
            ser.p("gpsimd", lambda e, h, full=full, half=half, rows=rows:
                  e.dma_start(out=half,
                              in_=full[bass.DynSlice(h["par_g"] * rows, rows), :]),
                  inc=("LG", 16))
        QCP = [c["LG"]]

        def emit_ln(src, dst, ntok):
            ser.begin()
            xf2 = [WF[:, 0:8192], WF[:, 8192:16384]]
            ab2 = [WF[:, 16384:16896], WF[:, 16896:17408]]
            bb2 = [WF[:, 17408:17920], WF[:, 17920:18432]]
            tm2 = [WF[:, 18432:18944], WF[:, 18944:19456]]
            xb2 = [WB[:, 0:8192], WB[:, 8192:16384]]
            xs2 = [WB[:, 16384:24576], WB[:, 24576:32768]]
            st = {}
            prow_free = {}
            pb_free = {}
            nb = ntok // 512
            for blk in range(nb):
                pr = blk % 2
                xf, ab, bb, tm = xf2[pr], ab2[pr], bb2[pr], tm2[pr]
                xb, xs = xb2[pr], xs2[pr]
                w0 = blk * 512
                w = [("S", st[blk - 2])] if blk >= 2 else []
                ser.p("sync", lambda e, h, w0=w0, xf=xf: e.dma_start(
                    out=xf.rearrange("p (k n) -> p k n", k=KT),
                    in_=src.rearrange("(k p) t -> p k t", p=128)[:, :, w0:w0 + 512]),
                    waits=w, inc=("L", 16))
                ldb = c["L"]
                ser.p("vector", lambda e, h, xb=xb, xf=xf: e.tensor_copy(xb[:], xf[:]),
                      waits=[("L", ldb)], inc=("F", 1))
                f_cast = c["F"]
                ser.p("scalar", lambda e, h, xs=xs, xf=xf: e.activation(
                    xs[:], xf[:], AF.Square), waits=[("L", ldb)], inc=("FS", 1))
                fs_sq = c["FS"]
                for kt in range(KT):
                    wm = []
                    if kt == 0:
                        wm = [("F", max(f_cast, prow_free.get(blk - 1, 0)))]
                    ser.p("tensor", lambda e, h, kt=kt, xb=xb: e.matmul(
                        P_ROW[:, :], ones_c[:], xb[:, kt * 512:(kt + 1) * 512],
                        start=(kt == 0), stop=(kt == KT - 1)),
                        waits=wm, inc=(("M", 1) if kt == KT - 1 else None))
                m1 = c["M"]
                ser.p("vector", lambda e, h: e.tensor_scalar_mul(
                    rowbuf[:], P_ROW[:, :], 1.0 / C),
                    waits=[("M", m1), ("F", c["F"])], inc=("F", 1))
                f2 = c["F"]
                for kt in range(KT):
                    wm = []
                    if kt == 0:
                        wm = [("FS", fs_sq), ("F", f2)]
                    ser.p("tensor", lambda e, h, kt=kt, xs=xs: e.matmul(
                        P_ROW[:, :], ones_c[:], xs[:, kt * 512:(kt + 1) * 512],
                        start=(kt == 0), stop=(kt == KT - 1)),
                        waits=wm, inc=(("M", 1) if kt == KT - 1 else None))
                m2 = c["M"]
                ser.p("vector", lambda e, h: e.tensor_scalar_mul(
                    rowbuf2[:], P_ROW[:, :], 1.0 / C),
                    waits=[("M", m2), ("F", c["F"])], inc=("F", 1))
                prow_free[blk] = c["F"]
                ser.p("vector", lambda e, h: e.tensor_mul(rowbuf3[:], rowbuf[:], rowbuf[:]),
                      waits=[("F", c["F"])], inc=("F", 1))
                ser.p("vector", lambda e, h: e.tensor_sub(rowbuf2[:], rowbuf2[:], rowbuf3[:]),
                      waits=[("F", c["F"])], inc=("F", 1))
                f5 = c["F"]
                ser.p("scalar", lambda e, h: e.activation(
                    rowbuf2[:], rowbuf2[:], AF.Ln, bias=c_eps5[:]),
                    waits=[("F", f5), ("FS", c["FS"])], inc=("FS", 1))
                ser.p("scalar", lambda e, h: e.activation(
                    rowbuf2[:], rowbuf2[:], AF.Exp, scale=-0.5),
                    waits=[("FS", c["FS"])], inc=("FS", 1))
                fs3 = c["FS"]
                ser.p("vector", lambda e, h: e.tensor_copy(rowbf[:], rowbuf2[:]),
                      waits=[("FS", fs3), ("F", c["F"])], inc=("F", 1))
                f6 = c["F"]
                ser.p("tensor", lambda e, h: e.matmul(
                    P_B[:, :], ones_r[:], rowbf[:], start=True, stop=True),
                    waits=[("F", max(f6, pb_free.get(blk - 1, 0)))], inc=("M", 1))
                m3 = c["M"]
                ser.p("vector", lambda e, h, ab=ab: e.tensor_copy(ab[:], P_B[:, :]),
                      waits=[("M", m3), ("F", c["F"])], inc=("F", 1))
                f7 = c["F"]
                ser.p("vector", lambda e, h: e.tensor_mul(rowbuf3[:], rowbuf[:], rowbuf2[:]),
                      waits=[("F", c["F"])], inc=("F", 1))
                ser.p("vector", lambda e, h: e.tensor_scalar_mul(rowbuf3[:], rowbuf3[:], -1.0),
                      waits=[("F", c["F"])], inc=("F", 1))
                ser.p("vector", lambda e, h: e.tensor_copy(rowbf[:], rowbuf3[:]),
                      waits=[("F", c["F"])], inc=("F", 1))
                f10 = c["F"]
                ser.p("tensor", lambda e, h: e.matmul(
                    P_B[:, :], ones_r[:], rowbf[:], start=True, stop=True),
                    waits=[("F", max(f10, f7))], inc=("M", 1))
                m4 = c["M"]
                ser.p("vector", lambda e, h, bb=bb: e.tensor_copy(bb[:], P_B[:, :]),
                      waits=[("M", m4), ("F", c["F"])], inc=("F", 1))
                pb_free[blk] = c["F"]
                for kt in range(KT):
                    sl = slice(kt * 512, (kt + 1) * 512)
                    ser.p("vector", lambda e, h, sl=sl, tm=tm, xf=xf, ab=ab:
                          e.tensor_mul(tm[:], xf[:, sl], ab[:]),
                          waits=([("F", c["F"])] if kt == 0 else []), inc=("F", 1))
                    ser.p("vector", lambda e, h, sl=sl, tm=tm, xs=xs, bb=bb:
                          e.tensor_add(xs[:, sl], tm[:], bb[:]), inc=("F", 1))
                f12 = c["F"]
                ser.p("sync", lambda e, h, w0=w0, xs=xs: e.dma_start(
                    out=dst.rearrange("(k p) t -> p k t", p=128)[:, :, w0:w0 + 512],
                    in_=xs.rearrange("p (k n) -> p k n", k=KT)),
                    waits=[("F", f12)], inc=("S", 16))
                st[blk] = c["S"]
            ser.end()

        emit_ln(xT, s_ln1, T)

        def sec_qk():
            ser.begin()
            eb0 = c["E"]
            WR = WB[:, 0:32768]
            rb = [WF[:, 0:4096].bitcast(BF16), WF[:, 4096:8192].bitcast(BF16)]
            obufs = [WB[:, 32768:33280], WB[:, 33280:33792]]
            st_box = {}
            mm_box = {}
            NMT = 16
            for mt in range(NMT):
                ser.p("scalar", lambda e, h, mt=mt: e.dma_start(
                    out=WR[:, mt * 2048:(mt + 1) * 2048], in_=s_wab[mt]),
                    waits=([("LG", PC["wab"])] if mt == 0 else []),
                    inc=("LS", 16))
            w_all = c["LS"]
            for n in range(8):
                rbuf = rb[n % 2]
                w = [("M", mm_box[(n - 2) * NMT + NMT - 1])] if n >= 2 else []
                ser.p("sync", lambda e, h, n=n, rbuf=rbuf: e.dma_start(
                    out=rbuf.rearrange("p (k t) -> p k t", k=KT),
                    in_=s_ln1.rearrange("(k p) t -> p k t", p=128)[:, :, n * 512:(n + 1) * 512]),
                    waits=w, inc=("L", 16))
                r_abs = c["L"]
                for mt in range(NMT):
                    u = n * NMT + mt
                    wt = WR[:, mt * 2048:(mt + 1) * 2048]
                    bank = PB[u % 3]
                    for kt in range(KT):
                        ww = []
                        if kt == 0:
                            ww = [("LS", w_all), ("L", r_abs)]
                            if u >= 3:
                                ww.append(("E", eb0 + u - 2))
                        ser.p("tensor", lambda e, h, kt=kt, wt=wt, bank=bank, rbuf=rbuf:
                              e.matmul(bank[:, :], wt[:, kt * 128:(kt + 1) * 128],
                                       rbuf[:, kt * 512:(kt + 1) * 512],
                                       start=(kt == 0), stop=(kt == KT - 1)),
                              waits=ww, inc=(("M", 1) if kt == KT - 1 else None))
                    mm_box[u] = c["M"]
                    ob = obufs[u % 2]
                    wx = [("M", mm_box[u])]
                    if u >= 2:
                        wx.append(("S", st_box[u - 2]))
                    ser.p("vector", lambda e, h, ob=ob, bank=bank: e.tensor_copy(ob[:], bank[:, :]),
                          waits=wx, inc=("E", 1))
                    e_abs = c["E"]
                    dstd = s_q if mt < 8 else s_k
                    r0 = (mt % 8) * 128
                    ser.p("sync", lambda e, h, ob=ob, dstd=dstd, r0=r0, n=n: e.dma_start(
                        out=dstd[r0:r0 + 128, n * 512:(n + 1) * 512], in_=ob[:]),
                        waits=[("E", e_abs)], inc=("S", 16))
                    st_box[u] = c["S"]
            ser.end()

        sec_qk()

        def sec_v():
            ser.begin()
            eb0 = c["E"]
            wv = WB[:, 0:KT * 1024]
            for vmt in range(8):
                ser.p("sync", lambda e, h, vmt=vmt: e.dma_start(
                    out=wv.rearrange("p (k n) -> p k n", k=KT)[:, :, vmt * 128:(vmt + 1) * 128],
                    in_=s_wab[16 + vmt].rearrange("p (k m) -> p k m", m=128)),
                    waits=([("LG", PC["wab"])] if vmt == 0 else []),
                    inc=("L", 16))
            wv_abs = c["L"]
            lts = [WB[:, 16384:18432], WB[:, 18432:20480]]
            obufs = [WB[:, 20480:20992], WB[:, 20992:21504]]
            st_box = {}
            mm_box = {}
            for tm in range(T // 128):
                lt = lts[tm % 2]
                w = [("M", mm_box[(tm - 2) * 2 + 1])] if tm >= 2 else []
                ser.p("scalar", lambda e, h, tm=tm, lt=lt: e.dma_start(
                    out=lt.rearrange("p (k n) -> p k n", k=KT),
                    in_=s_ln1.rearrange("(k p) t -> p k t", p=128)[:, :, tm * 128:(tm + 1) * 128]),
                    waits=w, inc=("LS", 16))
                lt_a = c["LS"]
                for n in range(2):
                    u = tm * 2 + n
                    bank = PB[u % 3]
                    for kt in range(KT):
                        ww = []
                        if kt == 0:
                            ww = [("L", wv_abs), ("LS", lt_a)]
                            if u >= 3:
                                ww.append(("E", eb0 + u - 2))
                        ser.p("tensor", lambda e, h, kt=kt, n=n, lt=lt, bank=bank:
                              e.matmul(bank[:, :], lt[:, kt * 128:(kt + 1) * 128],
                                       wv[:, kt * 1024 + n * 512: kt * 1024 + (n + 1) * 512],
                                       start=(kt == 0), stop=(kt == KT - 1)),
                              waits=ww, inc=(("M", 1) if kt == KT - 1 else None))
                    mm_box[u] = c["M"]
                    ob = obufs[u % 2]
                    wx = [("M", mm_box[u])]
                    if u >= 2:
                        wx.append(("S", st_box[u - 2]))
                    ser.p("vector", lambda e, h, ob=ob, bank=bank: e.tensor_copy(ob[:], bank[:, :]),
                          waits=wx, inc=("E", 1))
                    e_abs = c["E"]
                    ser.p("sync", lambda e, h, ob=ob, tm=tm, n=n: e.dma_start(
                        out=s_vt[tm * 128:(tm + 1) * 128, n * 512:(n + 1) * 512], in_=ob[:]),
                        waits=[("E", e_abs)], inc=("S", 16))
                    st_box[u] = c["S"]
            ser.end()

        sec_v()

        g3r = WF[0:4, 0:256]
        g3c = WF[0:4, 256:257]
        sinv3 = sb("sinv3", [128, 4], F32)
        QMATS = [(s_w1h, qh1, 8, 5632, 0), (s_w2h, qh2d, 8, 5632, 1),
                 (s_w3h, qh3, 22, 2048, 2)]

        def make_p1_steps():
            wt2 = [WF[:, 1024:1024 + 5632], WF[:, 1024 + 5632:1024 + 11264]]
            ld = {}
            rd = {}
            steps = []
            g = 0
            for wsrc, qdst, nh, cols, j in QMATS:
                for i in range(nh):
                    def em(g=g, i=i, nh=nh, wsrc=wsrc, cols=cols, j=j):
                        w = [("F", rd[g - 2])] if g >= 2 else []
                        if g == 0:
                            w.append(("LG", QCP[0]))
                        ser.p("sync", lambda e, h, i=i, wsrc=wsrc,
                              wt=wt2[g % 2][:, 0:cols]: e.dma_start(
                                  out=wt, in_=wsrc[i * 128:(i + 1) * 128, :]),
                              waits=w, inc=("L", 16))
                        ld[g] = c["L"]
                        ser.p("vector", lambda e, h, wt=wt2[g % 2][:, 0:cols]:
                              e.reduce_sum(acc_c2[:], wt, axis=AX.X,
                                           apply_absolute_value=True),
                              waits=[("L", ld[g])], inc=("F", 1))
                        if i == 0:
                            ser.p("vector", lambda e, h: e.tensor_copy(
                                acc_c[:], acc_c2[:]),
                                waits=[("F", c["F"])], inc=("F", 1))
                        else:
                            ser.p("vector", lambda e, h: e.tensor_add(
                                acc_c[:], acc_c[:], acc_c2[:]),
                                waits=[("F", c["F"])], inc=("F", 1))
                        rd[g] = c["F"]
                        if i == nh - 1:
                            ser.p("sync", lambda e, h, j=j: e.dma_start(
                                out=cc4_in[:, j:j + 1], in_=acc_c[:],
                                allow_slow_non_contiguous=True),
                                waits=[("F", rd[g])], inc=("L", 16))
                    steps.append(em)
                    g += 1
            return steps

        p1_q = make_p1_steps()

        def sec_attn():
            ser.begin()
            qb = [WB[:, 0:4096], WB[:, 4096:8192]]
            kb = [WB[:, 8192:12288], WB[:, 12288:16384]]
            vb = [WB[:, 16384:20480], WB[:, 20480:24576]]
            e_s = [WB[:, 24576:25088], WB[:, 25088:25600],
                   WB[:, 25600:26112], WB[:, 26112:26624]]
            ao_s = [WB[:, 26624:27136], WB[:, 27136:27648]]
            bc_f = [WF[:, 0:512], WF[:, 512:1024]]
            hl_abs = {}
            pe_last = {}

            def emit_loads(hh, w):
                ser.p("sync", lambda e, h, hh=hh, q=qb[hh % 2]: e.dma_start(
                    out=q[:, :], in_=s_q[hh * 128:(hh + 1) * 128, :]), waits=w, inc=("L", 16))
                ser.p("sync", lambda e, h, hh=hh, k=kb[hh % 2]: e.dma_start(
                    out=k[:, :], in_=s_k[hh * 128:(hh + 1) * 128, :]), inc=("L", 16))
                ser.p("sync", lambda e, h, hh=hh, v=vb[hh % 2]: e.dma_start(
                    out=v.rearrange("p (b n) -> p b n", b=32),
                    in_=s_vt.rearrange("(b p) v -> p b v", p=128)[:, :, hh * 128:(hh + 1) * 128]),
                    inc=("L", 16))
                hl_abs[hh] = c["L"]

            emit_loads(0, [])
            emit_loads(1, [])
            exps = []
            gs_box = [0]
            buf_last_rs = {0: 0, 1: 0, 2: 0, 3: 0}
            ge = 0
            stt_abs = {}
            recip_abs = {}
            tail_abs = {}
            bc_abs = {}
            bcev_abs = {}
            st_abs = {}
            for hh in range(NHA):
                q, k, v = qb[hh % 2], kb[hh % 2], vb[hh % 2]
                for qg in range(8):
                    u = hh * 8 + qg
                    nkb = 4 * qg + 4
                    pv_a = rs_a = 0
                    pend = None

                    def emit_pv(eb, kbi, x_a):
                        nonlocal pv_a, rs_a
                        w = [x_a]
                        if kbi == 0 and u >= 1:
                            w.append(("E", stt_abs[u - 1]))
                        ser.p("tensor", lambda e, h, kbi=kbi, v=v, es=e_s[eb]:
                              e.matmul(PB[0][:, :], v[:, kbi * 128:(kbi + 1) * 128], es[:],
                                       start=(kbi == 0), stop=(kbi == nkb - 1)),
                              waits=w, inc=("M", 1))
                        pv_a = c["M"]
                        w = []
                        if kbi == 0 and u >= 1:
                            w.append(("F", tail_abs[u - 1]))
                        ser.p("tensor", lambda e, h, kbi=kbi, es=e_s[eb]:
                              e.matmul(P_ROW[:, :], ones_c[:], es[:],
                                       start=(kbi == 0), stop=(kbi == nkb - 1)),
                              waits=w, inc=("M", 1))
                        rs_a = c["M"]
                        buf_last_rs[eb] = rs_a

                    for kbi in range(nkb):
                        sbank = PB[1 + gs_box[0] % 3]
                        gs_box[0] += 1
                        w = [("L", hl_abs[hh])]
                        if len(exps) >= 3:
                            w.append(exps[-3])
                        ser.p("tensor", lambda e, h, kbi=kbi, qg=qg, sbank=sbank, k=k, q=q:
                              e.matmul(sbank[:, :], k[:, kbi * 128:(kbi + 1) * 128],
                                       q[:, qg * 512:(qg + 1) * 512], start=True, stop=True),
                              waits=w, inc=("M", 1))
                        sc_a = c["M"]
                        eb = ge % 4
                        ge += 1
                        w = [("M", sc_a)]
                        if buf_last_rs[eb]:
                            w.append(("M", buf_last_rs[eb]))
                        ser.p("scalar", lambda e, h, es=e_s[eb], sbank=sbank:
                              e.activation(es[:], sbank[:, :], AF.Exp, scale=INV_SQRT_D),
                              waits=w, inc=("FS", 1))
                        x_a = ("FS", c["FS"])
                        if kbi >= 4 * qg:
                            base = 128 * (4 * qg - kbi)
                            ser.p("gpsimd", lambda e, h, es=e_s[eb], base=base:
                                  e.affine_select(
                                      out=es.rearrange("p (q y) -> p q y", q=4),
                                      in_=es.rearrange("p (q y) -> p q y", q=4),
                                      pattern=[[128, 4], [1, 128]], compare_op=ALU.is_ge,
                                      fill=0.0, base=base, channel_multiplier=-1),
                                  waits=[x_a], inc=("FG", 1))
                            x_a = ("FG", c["FG"])
                        exps.append(x_a)
                        if pend is not None:
                            emit_pv(*pend)
                        pend = (eb, kbi, x_a)
                    emit_pv(*pend)
                    ser.p("vector", lambda e, h: e.reciprocal(rowbuf[:], P_ROW[:, :]),
                          waits=[("M", rs_a)] + ([("M", bc_abs[u - 1])] if u >= 1 else []),
                          inc=("F", 1))
                    recip_abs[u] = c["F"]
                    ser.p("vector", lambda e, h: e.tensor_copy(rowbfB[:], rowbuf[:]),
                          waits=[("F", c["F"])], inc=("F", 1))
                    cast_a = c["F"]
                    wbc = [("F", cast_a)]
                    if u >= 1:
                        wbc.append(("FS", bcev_abs[u - 1]))
                    ser.p("tensor", lambda e, h: e.matmul(P_B[:, :], ones_r[:], rowbfB[:],
                                                          start=True, stop=True),
                          waits=wbc, inc=("M", 1))
                    bc_a = c["M"]
                    bc_abs[u] = bc_a
                    ser.p("scalar", lambda e, h, bf=bc_f[u % 2]: e.copy(bf[:], P_B[:, :]),
                          waits=[("M", bc_a)], inc=("FS", 1))
                    bcev_abs[u] = c["FS"]
                    w = [("FS", bcev_abs[u]), ("M", pv_a)]
                    if u >= 2:
                        w.append(("S", st_abs[u - 2]))
                    ser.p("vector", lambda e, h, ao=ao_s[u % 2], bf=bc_f[u % 2]:
                          e.scalar_tensor_tensor(ao[:], PB[0][:, :], 1.0, bf[:],
                                                 ALU.mult, ALU.mult),
                          waits=w, inc=("E", 1))
                    stt_abs[u] = c["E"]
                    tail_abs[u] = recip_abs[u]
                    ser.p("sync", lambda e, h, ao=ao_s[u % 2], hh=hh, qg=qg: e.dma_start(
                        out=cc1_in[hh * 128:(hh + 1) * 128, qg * 512:(qg + 1) * 512],
                        in_=ao[:]), waits=[("E", stt_abs[u])], inc=("S", 16))
                    st_abs[u] = c["S"]
                    if p1_q:
                        p1_q.pop(0)()
                pe_last[hh] = c["M"]
                if hh + 2 < NHA:
                    emit_loads(hh + 2, [("M", pe_last[hh])])
            while p1_q:
                p1_q.pop(0)()
            ser.end()

        sec_attn()

        ser.cc(lambda e, h: e.collective_compute(
            "AllGather", ALU.bypass,
            replica_groups=[[0, 1], [2, 3], [4, 5], [6, 7]],
            ins=[cc1_in.opt()], outs=[cc1_out.opt()]))
        ser.cc(lambda e, h: e.collective_compute(
            "AllGather", ALU.bypass,
            replica_groups=[[0, 1], [2, 3], [4, 5], [6, 7]],
            ins=[cc4_in.opt()], outs=[cc4_out.opt()]))
        ser.d(lambda e, h: e.dma_start(
            out=g3r[0:3, :], in_=cc4_out.rearrange("p a -> a p")[0:3, :],
            allow_slow_non_contiguous=True))
        ser.v(lambda e, h: e.reduce_sum(g3c[0:3, :], g3r[0:3, :], axis=AX.X))
        ser.v(lambda e, h: e.tensor_scalar(
            g3c[0:3, :], g3c[0:3, :], 1.0 / (C * HFF), 1e-8, ALU.mult, ALU.add))
        ser.v(lambda e, h: e.reciprocal(g3c[0:3, :], g3c[0:3, :]))
        ser.d(lambda e, h: e.dma_start(out=s_gq3[0:3, :], in_=g3c[0:3, :]))
        for j in range(3):
            ser.d(lambda e, h, j=j: e.dma_start(
                out=sinv3[:, j:j + 1], in_=s_gq3[j:j + 1, :].partition_broadcast(128)))

        def sec_nproj(kk, w_segs, rhs_dram, old_dram, new_dram, lgw=None):
            ser.begin()
            wta = WW[:, 0:kk * 128]
            wtb = WW[:, kk * 128:2 * kk * 128]
            hb = WB[:, 0:kk * 512]
            obufs = [WF[:, 0:512], WF[:, 512:1024]]
            olbufs = [WF[:, 1024:1536], WF[:, 1536:2048]]
            st_box = {}
            mm_box = {}
            ev_box = {}
            for n in range(4):
                w = [("M", mm_box[(n - 1) * 16 + 15])] if n >= 1 else []
                ser.p("sync", lambda e, h, n=n: e.dma_start(
                    out=hb.rearrange("p (k t) -> p k t", k=kk),
                    in_=rhs_dram.rearrange("(k p) t -> p k t", p=128)[:, :, n * 512:(n + 1) * 512]),
                    waits=w, inc=("L", 16))
                hb_abs = c["L"]
                for mt in range(16):
                    u = n * 16 + mt
                    wt = (wta, wtb)[u % 2]
                    wl = [("M", mm_box[u - 2])] if u >= 2 else []
                    co = 0
                    for si, (wsrc3, cw) in enumerate(w_segs):
                        wle = list(wl) if si == 0 else []
                        if si == 0 and u == 0 and lgw is not None:
                            wle.append(("LG", lgw))
                        ser.p("scalar", lambda e, h, mt=mt, wt=wt, co=co,
                              wsrc3=wsrc3, cw=cw: e.dma_start(
                                  out=wt[:, co:co + cw], in_=wsrc3[mt]),
                              waits=wle, inc=("LS", 16))
                        co += cw
                    ld_a = c["LS"]
                    olb = olbufs[u % 2]
                    wl2 = [("E", ev_box[u - 2])] if u >= 2 else []
                    ser.p("gpsimd", lambda e, h, olb=olb, mt=mt, n=n: e.dma_start(
                        out=olb[:], in_=old_dram[mt * 128:(mt + 1) * 128, n * 512:(n + 1) * 512]),
                        waits=wl2, inc=("LG", 16))
                    lo_a = c["LG"]
                    bank = PB[u % 3]
                    for kt in range(kk):
                        ww = []
                        if kt == 0:
                            ww = [("L", hb_abs), ("LS", ld_a)]
                            if u >= 3:
                                ww.append(("E", ev_box[u - 3]))
                        ser.p("tensor", lambda e, h, kt=kt, wt=wt, bank=bank:
                              e.matmul(bank[:, :], wt[:, kt * 128:(kt + 1) * 128],
                                       hb[:, kt * 512:(kt + 1) * 512],
                                       start=(kt == 0), stop=(kt == kk - 1)),
                              waits=ww, inc=(("M", 1) if kt == kk - 1 else None))
                    mm_box[u] = c["M"]
                    ob = obufs[u % 2]
                    w = [("M", mm_box[u]), ("LG", lo_a)]
                    if u >= 2:
                        w.append(("S", st_box[u - 2]))
                    ser.p("vector", lambda e, h, ob=ob, olb=olb, bank=bank: e.tensor_add(
                        ob[:], olb[:], bank[:, :]), waits=w, inc=("E", 1))
                    ev_box[u] = c["E"]
                    ser.p("sync", lambda e, h, ob=ob, mt=mt, n=n: e.dma_start(
                        out=new_dram[mt * 128:(mt + 1) * 128, n * 512:(n + 1) * 512],
                        in_=ob[:]), waits=[("E", ev_box[u])], inc=("S", 16))
                    st_box[u] = c["S"]
            ser.end()

        ser.d(lambda e, h: e.dma_start(out=s_ao, in_=cc1_out[:, bass.ts(h["par"], TOWN)]))
        ser.d(lambda e, h: e.dma_start(out=s_xo, in_=xT[:, bass.ts(h["par"], TOWN)]))
        sec_nproj(KT, [(s_wpb, 2048)], s_ao, s_xo, resid1, lgw=PC["wpb"])

        emit_ln(resid1, s_ln2, TOWN)



        def emit_quant_p2():
            ser.begin()
            wt2 = [WF[:, 1024:1024 + 5632], WF[:, 1024 + 5632:1024 + 11264]]
            qt2 = [WB[:, 0:5632], WB[:, 5632:11264]]
            qv = {}
            st = {}
            g = 0
            for wsrc, qdst, nh, cols, j in QMATS:
                qdst3 = qdst.rearrange("a (p m) -> a p m", p=128)
                for i in range(nh):
                    w = [("F", qv[g - 2])] if g >= 2 else []
                    ser.p("sync", lambda e, h, i=i, wsrc=wsrc,
                          wt=wt2[g % 2][:, 0:cols]: e.dma_start(
                              out=wt, in_=wsrc[i * 128:(i + 1) * 128, :]),
                          waits=w, inc=("L", 16))
                    lk = c["L"]
                    wt = wt2[g % 2][:, 0:cols]
                    qt = qt2[g % 2][:, 0:cols]
                    wi = WB[:, 11264 + (g % 2) * 11264:
                            11264 + (g % 2) * 11264 + 2 * cols].bitcast(I32)
                    ser.p("scalar", lambda e, h, wi=wi, wt=wt, j=j: e.activation(
                        wi, wt, AF.Copy, scale=sinv3[:, j:j + 1]),
                        waits=[("L", lk)], inc=("FS", 1))
                    ser.p("vector", lambda e, h, wi=wi: e.tensor_scalar(
                        wi, wi, 1, -1, ALU.min, ALU.max),
                        waits=[("FS", c["FS"]), ("F", c["F"])], inc=("F", 1))
                    wv = [("F", c["F"])]
                    if g >= 2:
                        wv.append(("S", st[g - 2]))
                    ser.p("vector", lambda e, h, wi=wi, qt=qt: e.tensor_copy(qt, wi),
                          waits=wv, inc=("F", 1))
                    qv[g] = c["F"]
                    ser.p("sync", lambda e, h, i=i, qt=qt, qdst3=qdst3: e.dma_start(
                        out=qdst3[:, :, i * 128:(i + 1) * 128].rearrange("a p m -> p a m"),
                        in_=qt.rearrange("p (a m) -> p a m", m=128)),
                        waits=[("F", qv[g])], inc=("S", 16))
                    st[g] = c["S"]
                    g += 1
            ser.end()

        emit_quant_p2()
        gq0 = c["G"]
        for ci, (qi, qo) in enumerate(((qh1, qg1), (qh2d, qg2), (qh3, qg3))):
            ser.p("gpsimd", lambda e, h, qi=qi, qo=qo: e.collective_compute(
                "AllGather", ALU.bypass,
                replica_groups=[[0, 1], [2, 3], [4, 5], [6, 7]],
                ins=[qi.opt()], outs=[qo.opt()]),
                waits=([("G", gq0)] if ci == 0 and gq0 else []), inc=("FG", 1))
        ser.p("sync", lambda e, h: e.nop(), waits=[("FG", c["FG"])], inc=("G", 1))

        def sec_ffn_hidden(ln_dram):
            ser.begin()
            rr = WB[:, 0:KT * 2048]
            ser.p("sync", lambda e, h: e.dma_start(
                out=rr.rearrange("p (k t) -> p k t", k=KT),
                in_=ln_dram.rearrange("(k p) t -> p k t", p=128)), inc=("L", 16))
            r_abs = c["L"]
            sigs = [WB[:, 32768:33280], WB[:, 33280:33792]]
            hids = [WB[:, 33792:34304], WB[:, 34304:34816]]
            st_box = {}
            mm_box = {}
            sig_box = {}
            stt_box = {}
            for u in range(44 * 4):
                hm, n = divmod(u, 4)
                wt1 = (WW[:, 0:2048], WW[:, 2048:4096])[hm % 2]
                wt2 = (WW[:, 4096:6144], WW[:, 6144:8192])[hm % 2]
                if n == 0:
                    wl = [("M", mm_box[(hm - 2) * 4 + 3])] if hm >= 2 else []
                    qg1v = qg1.rearrange("a (p m) -> a p m", p=128)
                    qg2v = qg2.rearrange("a (p m) -> a p m", p=128)
                    ser.p("sync", lambda e, h, hm=hm, wt1=wt1, qg1v=qg1v: e.dma_start(
                        out=wt1[:, 0:1024], in_=qg1v[hm]),
                        waits=wl, inc=("L", 16))
                    ser.p("sync", lambda e, h, hm=hm, wt1=wt1, qg1v=qg1v: e.dma_start(
                        out=wt1[:, 1024:2048], in_=qg1v[44 + hm]),
                        inc=("L", 16))
                    ser.p("scalar", lambda e, h, hm=hm, wt2=wt2, qg2v=qg2v: e.dma_start(
                        out=wt2[:, 0:1024], in_=qg2v[hm]),
                        waits=wl, inc=("LS", 16))
                    ser.p("scalar", lambda e, h, hm=hm, wt2=wt2, qg2v=qg2v: e.dma_start(
                        out=wt2[:, 1024:2048], in_=qg2v[44 + hm]),
                        inc=("LS", 16))
                ld_a = c["L"]
                ls_a = c["LS"]
                b1, b2 = PB[(u % 3) * 2], PB[(u % 3) * 2 + 1]
                for kt in range(KT):
                    ww = []
                    if kt == 0:
                        ww = [("L", max(r_abs, ld_a))]
                        if u >= 3:
                            ww.append(("FS", sig_box[u - 3]))
                    ser.p("tensor", lambda e, h, kt=kt, wt1=wt1, b1=b1, n=n:
                          e.matmul(b1[:, :], wt1[:, kt * 128:(kt + 1) * 128],
                                   rr[:, kt * 2048 + n * 512: kt * 2048 + (n + 1) * 512],
                                   start=(kt == 0), stop=(kt == KT - 1)),
                          waits=ww, inc=None)
                for kt in range(KT):
                    ww = []
                    if kt == 0:
                        ww = [("LS", ls_a)]
                        if u >= 3:
                            ww.append(("E", stt_box[u - 3]))
                    ser.p("tensor", lambda e, h, kt=kt, wt2=wt2, b2=b2, n=n:
                          e.matmul(b2[:, :], wt2[:, kt * 128:(kt + 1) * 128],
                                   rr[:, kt * 2048 + n * 512: kt * 2048 + (n + 1) * 512],
                                   start=(kt == 0), stop=(kt == KT - 1)),
                          waits=ww, inc=(("M", 1) if kt == KT - 1 else None))
                mm_box[u] = c["M"]
                sig = sigs[u % 2]
                hid = hids[u % 2]
                ser.p("scalar", lambda e, h, sig=sig, b1=b1: e.activation(
                    sig[:], b1[:, :], AF.Sigmoid), waits=[("M", mm_box[u])], inc=("FS", 1))
                sig_box[u] = c["FS"]
                w = [("FS", sig_box[u])]
                if u >= 2:
                    w.append(("S", st_box[u - 2]))
                ser.p("vector", lambda e, h, hid=hid, sig=sig, b2=b2:
                      e.scalar_tensor_tensor(hid[:], b2[:, :], 1.0, sig[:],
                                             ALU.mult, ALU.mult),
                      waits=w, inc=("E", 1))
                stt_box[u] = c["E"]
                ser.p("sync", lambda e, h, hid=hid, hm=hm, n=n: e.dma_start(
                    out=s_hid[hm * 128:(hm + 1) * 128, n * 512:(n + 1) * 512], in_=hid[:]),
                    waits=[("E", stt_box[u])], inc=("S", 16))
                st_box[u] = c["S"]
            ser.end()

        sec_ffn_hidden(s_ln2)
        qg3v = qg3.rearrange("a (p m) -> a p m", p=128)
        sec_nproj(HFF // 128, [(qg3v[0:KT], 2816), (qg3v[KT:2 * KT], 2816)],
                  s_hid, resid1, resid2)

        emit_ln(resid2, cc2_in, TOWN)
        ser.cc(lambda e, h: e.collective_compute(
            "AllGather", ALU.bypass,
            replica_groups=[[0, 1], [2, 3], [4, 5], [6, 7]],
            ins=[cc2_in.opt()], outs=[cc2_out.opt()]))
        for r in range(2):
            ser.d(lambda e, h, r=r: e.dma_start(
                out=s_ln3g[:, r * TOWN:(r + 1) * TOWN], in_=cc2_out[r * C:(r + 1) * C, :]))

        def sec_qkvconv():
            ser.begin()
            RRA = WB[:, 0:32768]
            wts = [WW[:, 0:2048], WW[:, 2048:4096]]
            mxp = ([WB[:, 32768 + i * 2064:32768 + (i + 1) * 2064] for i in range(3)]
                   + [WF[:, 4096:5128].bitcast(BF16)])
            t_a = WF[:, 0:2048]
            t_b = WF[:, 2048:4096]
            mm_box = {}
            ev_box = {}
            f_ch = {}
            s_out = {}
            r_a = [0, 0]

            def emit_acts(hh):
                t0 = hh * 2048
                w = [("M", mm_box[31])] if hh == 1 else []
                ser.p("sync", lambda e, h, t0=t0: e.dma_start(
                    out=RRA.rearrange("p (k t) -> p k t", k=KT),
                    in_=s_ln3g.rearrange("(k p) t -> p k t", p=128)[:, :, t0:t0 + 2048]),
                    waits=w, inc=("L", 16))
                r_a[hh] = c["L"]

            def emit_proj(u):
                hh, ct = u // 32, u % 32
                wt = wts[u % 2]
                wl = [("M", mm_box[u - 2])] if u >= 2 else []
                if u == 0:
                    wl = wl + [("LG", PC["wqb"])]
                ser.p("scalar", lambda e, h, ct=ct, wt=wt: e.dma_start(
                    out=wt[:], in_=s_wqb[ct]),
                    waits=wl, inc=("LS", 16))
                w_a = c["LS"]
                for n in range(4):
                    v = u * 4 + n
                    bank = PB[v % 4]
                    for kt in range(KT):
                        ww = []
                        if kt == 0:
                            ww = [("LS", w_a), ("L", r_a[hh])]
                            if v >= 4:
                                ww.append(("FS", ev_box[v - 4]))
                            if n == 0 and u >= 4:
                                ww.append(f_ch[u - 4])
                        ser.p("tensor", lambda e, h, kt=kt, wt=wt, bank=bank, n=n:
                              e.matmul(bank[:, :], wt[:, kt * 128:(kt + 1) * 128],
                                       RRA[:, kt * 2048 + n * 512:kt * 2048 + (n + 1) * 512],
                                       start=(kt == 0), stop=(kt == KT - 1)),
                              waits=ww, inc=(("M", 1) if kt == KT - 1 else None))
                    mm_box[u] = c["M"]
                    ser.p("scalar", lambda e, h, mx=mxp[u % 4], n=n, bank=bank:
                          e.copy(mx[:, 16 + n * 512:16 + (n + 1) * 512], bank[:, :]),
                          waits=[("M", mm_box[u])], inc=("FS", 1))
                    ev_box[v] = c["FS"]

            def emit_chain(u):
                hh, ct = u // 32, u % 32
                t0 = hh * 2048
                mx = mxp[u % 4]
                prev = [None]

                def step(eng, fn, extra=(), cname=None):
                    w = list(extra)
                    if prev[0] is not None and prev[0][0] != cname:
                        w.append(prev[0])
                    cn = {"vector": "F", "scalar": "FS", "gpsimd": "FG",
                          "tensor": "M", "sync": "S"}[eng]
                    ser.p(eng, fn, waits=w, inc=(cn, 16 if eng == "sync" else 1))
                    prev[0] = (cn, c[cn])

                w0 = [("FS", ev_box[u * 4 + 3])]
                if u >= 1:
                    w0.append(f_ch[u - 1])
                if hh == 0:
                    step("vector", lambda e, h, mx=mx: e.memset(mx[:, 13:16], 0.0),
                         extra=w0, cname="X")
                else:
                    w0.append(("S", s_out[ct]))
                    step("sync", lambda e, h, mx=mx, ct=ct: e.dma_start(
                        out=mx[:, 13:16], in_=s_mx[ct * 128:(ct + 1) * 128, 2045:2048]),
                        extra=w0, cname="X")
                step("vector", lambda e, h, mx=mx, ct=ct: e.tensor_scalar_mul(
                    t_a[:], mx[:, 16:16 + 2048], cw_s[:, ct * 4 + 3:ct * 4 + 4]),
                    cname="F")
                step("vector", lambda e, h, mx=mx, ct=ct: e.scalar_tensor_tensor(
                    t_b[:], mx[:, 15:15 + 2048], cw_s[:, ct * 4 + 2:ct * 4 + 3],
                    t_a[:], ALU.mult, ALU.add), cname="F")
                step("vector", lambda e, h, mx=mx, ct=ct: e.scalar_tensor_tensor(
                    t_a[:], mx[:, 14:14 + 2048], cw_s[:, ct * 4 + 1:ct * 4 + 2],
                    t_b[:], ALU.mult, ALU.add), cname="F")
                step("vector", lambda e, h, mx=mx, ct=ct: e.scalar_tensor_tensor(
                    t_b[:], mx[:, 13:13 + 2048], cw_s[:, ct * 4 + 0:ct * 4 + 1],
                    t_a[:], ALU.mult, ALU.add), cname="F")
                step("scalar", lambda e, h, mx=mx: e.activation(
                    mx[:, 16:16 + 2048], t_b[:], AF.Silu), cname="X")
                step("sync", lambda e, h, mx=mx, ct=ct, t0=t0: e.dma_start(
                    out=s_mx[ct * 128:(ct + 1) * 128, t0:t0 + 2048],
                    in_=mx[:, 16:16 + 2048]),
                    cname="X")
                f_ch[u] = prev[0]
                if hh == 0:
                    s_out[ct] = c["S"]

            emit_acts(0)
            for u in range(65):
                if u == 32:
                    emit_acts(1)
                if u < 64:
                    emit_proj(u)
                if u >= 1:
                    emit_chain(u - 1)
            ser.end()

        sec_qkvconv()

        def sec_l2():
            ser.begin()
            mbuf = [WB[:, 0:4096], WB[:, 4096:8192]]
            sqb = [WB[:, 8192:12288], WB[:, 12288:16384]]
            qnb = [WB[:, 16384:16896], WB[:, 16896:17408]]
            rows_f = [WF[:, 0:4096][0:1, :], WF[:, 4096:8192][0:1, :]]
            rows_b = [WB[:, 17408:21504][0:1, :], WB[:, 21504:25600][0:1, :]]
            ld_box = {}
            sq_box = {}
            ch_last = {}
            qst = []
            for ct in range(16):
                mb = mbuf[ct % 2]
                w = [("F", ch_last[ct - 2])] if ct >= 2 else []
                ser.p("sync", lambda e, h, ct=ct, mb=mb: e.dma_start(
                    out=mb[:, :], in_=s_mx[ct * 128:(ct + 1) * 128, :]),
                    waits=w, inc=("L", 16))
                ld_box[ct] = c["L"]
                w = [("L", ld_box[ct])]
                if ct >= 2:
                    w.append(("FS", sq_box[ct - 2]))
                ser.p("scalar", lambda e, h, ct=ct, mb=mb, sq=sqb[ct % 2]:
                      e.activation(sq[:], mb[:, :], AF.Square), waits=w, inc=("FS", 1))
                sq_box[ct] = c["FS"]
                rf = rows_f[ct % 2]
                rb_ = rows_b[ct % 2]
                prevrow = 0
                for n in range(8):
                    w = [("FS", sq_box[ct])]
                    if n == 0 and ct >= 1:
                        w.append(("F", ch_last[ct - 1]))
                    if prevrow:
                        w.append(("F", prevrow))
                    ser.p("tensor", lambda e, h, n=n, sq=sqb[ct % 2]: e.matmul(
                        P_ROW[:, :], ones_c[:], sq[:, n * 512:(n + 1) * 512],
                        start=True, stop=True), waits=w, inc=("M", 1))
                    ser.p("vector", lambda e, h, n=n, rf=rf: e.tensor_copy(
                        rf[0:1, n * 512:(n + 1) * 512], P_ROW[:, :]),
                        waits=[("M", c["M"])], inc=("F", 1))
                    prevrow = c["F"]
                ser.p("scalar", lambda e, h, rf=rf: e.activation(
                    rf[:], rf[:], AF.Ln, bias=c_eps6[:]),
                    waits=[("F", prevrow)], inc=("FS", 1))
                if ct < 8:
                    ser.p("scalar", lambda e, h, rf=rf: e.activation(
                        rf[:], rf[:], AF.Exp, scale=-0.5, bias=c_lnq[:]),
                        waits=[("FS", c["FS"])], inc=("FS", 1))
                else:
                    ser.p("scalar", lambda e, h, rf=rf: e.activation(
                        rf[:], rf[:], AF.Exp, scale=-0.5),
                        waits=[("FS", c["FS"])], inc=("FS", 1))
                ser.p("vector", lambda e, h, rf=rf, rb_=rb_: e.tensor_copy(rb_[:], rf[:]),
                      waits=[("FS", c["FS"])], inc=("F", 1))
                cast_a = c["F"]
                prev_stt = ch_last.get(ct - 1, 0)
                for n in range(8):
                    w = [("F", cast_a)]
                    if prev_stt:
                        w.append(("F", prev_stt))
                    ser.p("tensor", lambda e, h, n=n, rb_=rb_: e.matmul(
                        P_B[:, :], ones_r[:], rb_[0:1, n * 512:(n + 1) * 512],
                        start=True, stop=True), waits=w, inc=("M", 1))
                    bca = c["M"]
                    w2 = [("M", bca)]
                    if len(qst) >= 2:
                        w2.append(("S", qst[-2]))
                    ser.p("vector", lambda e, h, n=n, mb=mb, qn=qnb[n % 2]:
                          e.scalar_tensor_tensor(qn[:], P_B[:, :], 1.0,
                                                 mb[:, n * 512:(n + 1) * 512],
                                                 ALU.mult, ALU.mult),
                          waits=w2, inc=("F", 1))
                    prev_stt = c["F"]
                    ser.p("sync", lambda e, h, ct=ct, n=n, qn=qnb[n % 2]: e.dma_start(
                        out=s_mx[ct * 128:(ct + 1) * 128, n * 512:(n + 1) * 512],
                        in_=qn[:]), waits=[("F", c["F"])], inc=("S", 16))
                    qst.append(c["S"])
                ch_last[ct] = c["F"]
            ser.end()

        sec_l2()

        def sec_z():
            ser.begin()
            wz = WB[:, 0:KT * 2048]
            for zmt in range(16):
                ser.p("sync", lambda e, h, zmt=zmt: e.dma_start(
                    out=wz.rearrange("p (k n) -> p k n", k=KT)[:, :, zmt * 128:(zmt + 1) * 128],
                    in_=s_wzb[zmt].rearrange("p (k m) -> p k m", m=128)),
                    waits=([("LG", PC["wzb"])] if zmt == 0 else []),
                    inc=("L", 16))
            wz_abs = c["L"]
            lts = [WB[:, 32768:34816], WB[:, 34816:36864]]
            obufs = [WB[:, 36864:37376], WB[:, 37376:37888]]
            st_box = {}
            mm_box = {}
            ev_box = {}
            for tm in range(T // 128):
                lt = lts[tm % 2]
                w = [("M", mm_box[(tm - 2) * 4 + 3])] if tm >= 2 else []
                ser.p("scalar", lambda e, h, tm=tm, lt=lt: e.dma_start(
                    out=lt.rearrange("p (k n) -> p k n", k=KT),
                    in_=s_ln3g.rearrange("(k p) t -> p k t", p=128)[:, :, tm * 128:(tm + 1) * 128]),
                    waits=w, inc=("LS", 16))
                lt_a = c["LS"]
                for zn in range(4):
                    u = tm * 4 + zn
                    bank = PB[u % 3]
                    for kt in range(KT):
                        ww = []
                        if kt == 0:
                            ww = [("L", wz_abs), ("LS", lt_a)]
                            if u >= 3:
                                ww.append(("E", ev_box[u - 3]))
                        ser.p("tensor", lambda e, h, kt=kt, lt=lt, bank=bank, zn=zn:
                              e.matmul(bank[:, :], lt[:, kt * 128:(kt + 1) * 128],
                                       wz[:, kt * 2048 + zn * 512: kt * 2048 + (zn + 1) * 512],
                                       start=(kt == 0), stop=(kt == KT - 1)),
                              waits=ww, inc=(("M", 1) if kt == KT - 1 else None))
                    mm_box[u] = c["M"]
                    ob = obufs[u % 2]
                    w = [("M", mm_box[u])]
                    if u >= 2:
                        w.append(("S", st_box[u - 2]))
                    ser.p("vector", lambda e, h, ob=ob, bank=bank: e.tensor_copy(ob[:], bank[:, :]),
                          waits=w, inc=("E", 1))
                    ev_box[u] = c["E"]
                    ser.p("sync", lambda e, h, ob=ob, tm=tm, zn=zn: e.dma_start(
                        out=s_zt[tm * 128:(tm + 1) * 128, zn * 512:(zn + 1) * 512], in_=ob[:]),
                        waits=[("E", ev_box[u])], inc=("S", 16))
                    st_box[u] = c["S"]
            ser.end()

        sec_z()

        def emit_ba():
            wb_t = WW[:, 0:KT * 16]
            wa_t = WW[:, KT * 16:KT * 32]
            rbuf = WB[:, 0:KT * 512]
            ser.d(lambda e, h: e.dma_start(
                out=wb_t.rearrange("p (k n) -> p k n", k=KT),
                in_=s_wbab.rearrange("(k p) n -> p k n", p=128)[:, :, 0:16]))
            ser.d(lambda e, h: e.dma_start(
                out=wa_t.rearrange("p (k n) -> p k n", k=KT),
                in_=s_wbab.rearrange("(k p) n -> p k n", p=128)[:, :, 16:32]))
            for n in range(8):
                ser.d(lambda e, h, n=n: e.dma_start(
                    out=rbuf.rearrange("p (k t) -> p k t", k=KT),
                    in_=s_ln3g.rearrange("(k p) t -> p k t", p=128)[:, :, n * 512:(n + 1) * 512]))
                for w_t, dstb in ((wb_t, GPb), (wa_t, GPa)):
                    for kt in range(KT):
                        ser.t(lambda e, h, kt=kt, w_t=w_t: e.matmul(
                            P_SM[0:16, :], w_t[:, kt * 16:(kt + 1) * 16],
                            rbuf[:, kt * 512:(kt + 1) * 512],
                            start=(kt == 0), stop=(kt == KT - 1)))
                    ser.v(lambda e, h, n=n, dstb=dstb: e.tensor_copy(
                        dstb[:, n * 512:(n + 1) * 512], P_SM[0:16, :]))

        ser.p("sync", lambda e, h: e.nop(), waits=[("LG", PC["wbab"])], inc=("G", 1))
        emit_ba()

        def emit_gates():
            ser.v(lambda e, h: e.tensor_scalar_add(GX[:, :], GPa[:, :], dtb_s[:]))
            ser.v(lambda e, h: e.tensor_scalar_mul(GPa[:, :], GX[:, :], -1.0))
            ser.v(lambda e, h: e.tensor_max(GS2[:, :], GX[:, :], GPa[:, :]))
            ser.s(lambda e, h: e.activation(GPa[:, :], GS2[:, :], AF.Exp, scale=-1.0))
            ser.s(lambda e, h: e.activation(GS2[:, :], GPa[:, :], AF.Ln, bias=c_one16[:]))
            ser.v(lambda e, h: e.tensor_relu(GPa[:, :], GX[:, :]))
            ser.v(lambda e, h: e.tensor_add(GX[:, :], GS2[:, :], GPa[:, :]))
            ser.v(lambda e, h: e.tensor_scalar_mul(GX[:, :], GX[:, :], negea[:]))
            ser.s(lambda e, h: e.activation(GPb[:, :], GPb[:, :], AF.Sigmoid))
            for cc_ in range(NCH):
                ser.t(lambda e, h, cc_=cc_: e.transpose(
                    PB[2][:, 0:16], GX[:, cc_ * 128:(cc_ + 1) * 128], ident_f[0:16, 0:16]))
                ser.v(lambda e, h, cc_=cc_: e.tensor_copy(gtm[:, cc_ * 16:(cc_ + 1) * 16],
                                                          PB[2][:, 0:16]))
                ser.t(lambda e, h, cc_=cc_: e.transpose(
                    PB[2][:, 0:16], GPb[:, cc_ * 128:(cc_ + 1) * 128], ident_f[0:16, 0:16]))
                ser.v(lambda e, h, cc_=cc_: e.tensor_copy(btm[:, cc_ * 16:(cc_ + 1) * 16],
                                                          PB[2][:, 0:16]))
            ser.t(lambda e, h: e.matmul(PB[2][:, :], triu_f[:], gtm[:], start=True, stop=True))
            ser.v(lambda e, h: e.tensor_copy(Gtm[:], PB[2][:, :]))
            ser.s(lambda e, h: e.activation(Btm[:], Gtm[:], AF.Exp))
            ser.v(lambda e, h: e.tensor_scalar_mul(negBtm[:], Btm[:], -1.0))
            ser.s(lambda e, h: e.activation(lnbtm[:], btm[:], AF.Ln))
            ser.v(lambda e, h: e.tensor_add(GLBtm[:], Gtm[:], lnbtm[:]))
            ser.d(lambda e, h: e.dma_start(out=glbrow4[:, 0:512],
                                           in_=Gtm[127:128, :]))
            ser.t(lambda e, h: e.matmul(PB[2][:, :], ones_rf[:], glbrow4[:, 0:512],
                                        start=True, stop=True))
            ser.v(lambda e, h: e.tensor_copy(eNGt[:], PB[2][:, :]))
            ser.s(lambda e, h: e.activation(BLtile[:], eNGt[:], AF.Exp))
            ser.v(lambda e, h: e.tensor_sub(eNGt[:], eNGt[:], Gtm[:]))
            ser.s(lambda e, h: e.activation(scolTm[:], eNGt[:], AF.Exp))

        emit_gates()

        S_f = [WF[:, 9248:11296], WF[:, 11296:13344]]
        BLtile = WF[:, 13344:13856]
        scolTm = WF[:, 13856:14368]
        eNGt = WF[:, 14368:14880]
        S_b = WB[:, 37888:39936]
        grow4 = sb("grow4", [1, 768], F32)
        negrow4 = sb("negrow4", [1, 768], F32)
        glbrow4 = sb("glbrow4", [1, 768], F32)
        scol4 = sb("scol4", [128, 6], F32)
        colGL4 = sb("colGL4", [128, 6], F32)
        colBL4 = sb("colBL4", [128, 6], F32)

        def sec_delta():
            ser.begin()
            qh2 = [WB[:, 0:1024], WB[:, 1024:2048]]
            kh2 = [WB[:, 2048:3072], WB[:, 3072:4096]]
            kt2 = [WB[:, 4096:5120], WB[:, 5120:6144]]
            vt2 = [WB[:, 6144:8192], WB[:, 8192:10240]]
            zt2 = [WB[:, 10240:12288], WB[:, 12288:14336]]
            slz2 = [WB[:, 14336:16384], WB[:, 16384:18432]]
            NSLOT = 6
            SLOT = []
            for sl in range(NSLOT):
                b0 = 18432 + sl * 2560
                SLOT.append(dict(
                    Mb=WB[:, b0:b0 + 128], Nb=WB[:, b0 + 128:b0 + 256],
                    Wtb=WB[:, b0 + 256:b0 + 384], Ua=WB[:, b0 + 384:b0 + 512],
                    Ub=WB[:, b0 + 512:b0 + 640],
                    Pow=[WB[:, b0 + 640 + i * 128: b0 + 768 + i * 128] for i in range(12)],
                    PowPair=[WB[:, b0 + 640 + i * 256: b0 + 896 + i * 256] for i in range(5)],
                    Kp=WB[:, b0 + 2176:b0 + 2304],
                    eDs=WF[:, sl * 128:(sl + 1) * 128],
                    eDt=WF[:, 768 + sl * 128: 768 + (sl + 1) * 128],
                    eDsT=WF[:, 1536 + sl * 128: 1536 + (sl + 1) * 128],
                    o1=WF[:, 2304 + sl * 128: 2304 + (sl + 1) * 128],
                    grow=grow4[:, sl * 128:(sl + 1) * 128],
                    negrow=negrow4[:, sl * 128:(sl + 1) * 128],
                    glbrow=glbrow4[:, sl * 128:(sl + 1) * 128],
                    scol=scol4[:, sl:sl + 1], colGL=colGL4[:, sl:sl + 1],
                    colBL=colBL4[:, sl:sl + 1],
                    p_row=PB[sl][0:1, 384:512],
                    p_bc=PB[sl][:, 448:449],
                    p_d=PB[sl][:, 128:256],
                    p_a=PB[sl][:, 0:128],
                    p_q=PB[sl][:, 256:384],
                    p_w=PB[sl][:, 0:128],
                    p_s=PB[sl][:, 0:128],
                    last=0,
                ))
            cnb2 = [WB[:, 33792:35840], WB[:, 35840:37888]]
            O_f2 = [WF[:, 3072:5120], WF[:, 5120:7168]]
            sq_f = WF[:, 7168:9216]
            rstd2 = [WF[:, 9216:9232], WF[:, 9232:9248]]
            ser.p("vector", lambda e, h: e.memset(S_f[0], 0.0), inc=("F", 1))
            ser.p("vector", lambda e, h: e.memset(S_b, 0.0), inc=("F", 1))
            init_f = ("F", c["F"])
            for sl in range(NSLOT):
                SLOT[sl]["last"] = init_f
            gate_done = {}
            chunk_last = {}

            def emit_unit(cci, vh, l_abs, prep_f, sl):
                S = SLOT[sl]
                cb = cci % 2
                cur, nxt = S_f[cci % 2], S_f[(cci + 1) % 2]
                kh = vh // 2
                cv = cci * 16 + vh
                qh = qh2[cb][:, kh * 128:(kh + 1) * 128]
                khh = kh2[cb][:, kh * 128:(kh + 1) * 128]
                Gc = Gtm[:, cv:cv + 1]
                Bc = Btm[:, cv:cv + 1]
                nBc = negBtm[:, cv:cv + 1]
                ops = []

                def T(fn, w=()):
                    ops.append(("tensor", fn, ("M", 1), w))

                def V(fn, w=()):
                    ops.append(("vector", fn, ("F", 1), w))

                def SS(fn, w=()):
                    ops.append(("scalar", fn, ("FS", 1), w))

                def G_(fn, w=()):
                    ops.append(("gpsimd", fn, ("FG", 1), w))

                T(lambda e, h: e.transpose(S["p_row"], Gc, ident_f[:]))
                SS(lambda e, h: e.copy(S["grow"], S["p_row"]))
                V(lambda e, h: e.tensor_scalar_mul(S["negrow"], S["grow"], -1.0))
                T(lambda e, h: e.matmul(S["p_d"], S["grow"], ones_rf[:],
                                        start=True, stop=False))
                T(lambda e, h: e.matmul(S["p_d"], ones_rf[:], S["negrow"],
                                        start=False, stop=True))
                SS(lambda e, h: e.activation(S["eDs"], S["p_d"], AF.Exp,
                                             bias=lnbtm[:, cv:cv + 1]))
                G_(lambda e, h: e.affine_select(
                    out=S["eDs"], in_=S["eDs"], pattern=[[-1, 128]], compare_op=ALU.is_ge,
                    fill=0.0, base=-1, channel_multiplier=1))
                SS(lambda e, h: e.activation(S["eDt"], S["p_d"], AF.Exp, scale=-1.0))
                G_(lambda e, h: e.affine_select(
                    out=S["eDt"], in_=S["eDt"], pattern=[[1, 128]], compare_op=ALU.is_ge,
                    fill=0.0, base=0, channel_multiplier=-1))
                T(lambda e, h: e.matmul(S["p_a"], khh, khh, start=True, stop=True),
                  w=[("L", l_abs)])
                T(lambda e, h: e.matmul(S["p_q"], khh, qh, start=True, stop=True))
                V(lambda e, h: e.tensor_tensor(S["Mb"], S["p_a"], S["eDs"], ALU.mult))
                V(lambda e, h: e.tensor_tensor(S["Wtb"], S["p_q"], S["eDt"], ALU.mult))
                T(lambda e, h: e.transpose(S["p_d"], S["eDs"], ident_f[:]))
                SS(lambda e, h: e.copy(S["eDsT"], S["p_d"]))
                V(lambda e, h: e.tensor_tensor(S["Nb"], S["p_a"], S["eDsT"], ALU.mult))
                Pc, Nc = S["Mb"], S["Nb"]
                pw2 = PB[sl][:, 0:256]
                for lvl in range(5):
                    Pn, Nn = S["Pow"][lvl * 2], S["Pow"][lvl * 2 + 1]
                    Pp = S["PowPair"][lvl]
                    T(lambda e, h, Nc=Nc, Pc=Pc: e.matmul(PB[sl][:, 0:128], Nc, Pc,
                                                          start=True, stop=True))
                    T(lambda e, h, Nc=Nc, Pc=Pc: e.matmul(PB[sl][:, 128:256], Pc, Nc,
                                                          start=True, stop=True))
                    if lvl % 2 == 0:
                        SS(lambda e, h, Pp=Pp: e.copy(Pp, pw2))
                    else:
                        V(lambda e, h, Pp=Pp: e.tensor_copy(Pp, pw2))
                    Pc, Nc = Pn, Nn
                N64 = S["Pow"][10]
                T(lambda e, h, Pc=Pc, Nc=Nc: e.matmul(PB[sl][:, 0:128], Pc, Nc,
                                                      start=True, stop=True))
                SS(lambda e, h, N64=N64: e.copy(N64[:], PB[sl][:, 0:128]))
                T(lambda e, h: e.matmul(S["p_s"], khh, S_b[:, vh * 128:(vh + 1) * 128],
                                        start=True, stop=True))
                V(lambda e, h: e.scalar_tensor_tensor(
                    S["o1"], S["p_s"], nBc, vt2[cb][:, vh * 128:(vh + 1) * 128],
                    ALU.mult, ALU.add))
                V(lambda e, h: e.tensor_scalar_mul(S["Ua"], S["o1"], btm[:, cv:cv + 1]))
                Ucur, Uoth = S["Ua"], S["Ub"]
                for Nl in [S["Pow"][10], S["Pow"][9], S["Pow"][7], S["Pow"][5],
                           S["Pow"][3], S["Pow"][1]]:
                    T(lambda e, h, Nl=Nl, Uc=Ucur: e.matmul(S["p_w"], Nl, Uc,
                                                            start=True, stop=True))
                    V(lambda e, h, Uc=Ucur, Uo=Uoth: e.tensor_add(Uo[:], S["p_w"], Uc[:]))
                    Ucur, Uoth = Uoth, Ucur
                T(lambda e, h, Uc=Ucur: e.matmul(S["p_w"], S["Nb"], Uc,
                                                 start=True, stop=True))
                V(lambda e, h, Uc=Ucur, Uo=Uoth: e.tensor_sub(Uo[:], Uc[:], S["p_w"]))
                Ucur, Uoth = Uoth, Ucur
                T(lambda e, h: e.matmul(S["p_s"], qh, S_b[:, vh * 128:(vh + 1) * 128],
                                        start=True, stop=True))
                V(lambda e, h: e.tensor_scalar_mul(S["o1"], S["p_s"], Bc))
                T(lambda e, h, Uc=Ucur: e.matmul(S["p_w"], S["Wtb"], Uc,
                                                 start=True, stop=True))
                V(lambda e, h, cci=cci: e.tensor_add(
                    O_f2[cci % 2][:, vh * 128:(vh + 1) * 128], S["p_w"], S["o1"]))
                o_write = True
                V(lambda e, h: e.tensor_scalar_mul(
                    S["Kp"], kt2[cb][:, kh * 128:(kh + 1) * 128],
                    scolTm[:, cv:cv + 1]))
                T(lambda e, h, Uc=Ucur: e.matmul(S["p_w"], S["Kp"], Uc,
                                                 start=True, stop=True))
                V(lambda e, h: e.scalar_tensor_tensor(
                    nxt[:, vh * 128:(vh + 1) * 128], cur[:, vh * 128:(vh + 1) * 128],
                    BLtile[:, cv:cv + 1], S["p_w"], ALU.mult, ALU.add))
                SS(lambda e, h: e.copy(S_b[:, vh * 128:(vh + 1) * 128],
                                       nxt[:, vh * 128:(vh + 1) * 128]))
                return ops

            def emit_group(cci, vhs, l_abs, prep_f):
                chains = []
                for i, vh in enumerate(vhs):
                    ops = emit_unit(cci, vh, l_abs, prep_f, i)
                    chains.append([ops, SLOT[i]["last"]])
                maxlen = max(len(ch[0]) for ch in chains)
                for k in range(maxlen):
                    for i in range(len(vhs)):
                        ops, prev = chains[i]
                        if k >= len(ops):
                            continue
                        eng, fn, inc, extra = ops[k]
                        w = list(extra)
                        if prev is not None:
                            w.append(prev)
                        ser.p(eng, fn, waits=w, inc=inc)
                        chains[i][1] = (inc[0], c[inc[0]])
                for i in range(len(vhs)):
                    SLOT[i]["last"] = chains[i][1]
                return [chains[i][1] for i in range(len(vhs))]

            for cci in range(NCH):
                t0 = cci * 128
                cb = cci % 2
                wl = [chunk_last[cci - 2]] if cci >= 2 else []
                ser.p("sync", lambda e, h, t0=t0, cb=cb: e.dma_start(
                    out=qh2[cb].rearrange("p (k t) -> p k t", k=NKH),
                    in_=s_mx.rearrange("(k p) t -> p k t", p=128)[:, 0:NKH, t0:t0 + 128]),
                    waits=wl, inc=("L", 16))
                ser.p("sync", lambda e, h, t0=t0, cb=cb: e.dma_start(
                    out=kh2[cb].rearrange("p (k t) -> p k t", k=NKH),
                    in_=s_mx.rearrange("(k p) t -> p k t", p=128)[:, NKH:2 * NKH, t0:t0 + 128]),
                    inc=("L", 16))
                ser.p("sync", lambda e, h, t0=t0, cb=cb: e.dma_start_transpose(
                    out=kt2[cb][:, :], in_=s_mx[1024:2048, t0:t0 + 128]), inc=("L", 16))
                ser.p("sync", lambda e, h, t0=t0, cb=cb: e.dma_start_transpose(
                    out=vt2[cb][:, :], in_=s_mx[2048:4096, t0:t0 + 128]), inc=("L", 16))
                ser.p("sync", lambda e, h, t0=t0, cb=cb: e.dma_start(
                    out=zt2[cb][:, :], in_=s_zt[t0:t0 + 128, :]), inc=("L", 16))
                l_abs = c["L"]
                ser.p("scalar", lambda e, h, cb=cb: e.activation(
                    slz2[cb][:], zt2[cb][:], AF.Silu),
                    waits=[("L", l_abs)] + wl, inc=("FS", 1))
                ser.p("vector", lambda e, h, cb=cb: e.tensor_tensor(
                    slz2[cb].rearrange("p (r n) -> p r n", r=NVH),
                    slz2[cb].rearrange("p (r n) -> p r n", r=NVH),
                    NWbc[:, :].unsqueeze(1).broadcast_to([128, NVH, 128]), ALU.mult),
                    waits=[("FS", c["FS"])], inc=("F", 1))
                prep_f = ("F", c["F"])
                for vhs in ([0, 1, 2, 3, 4, 5], [6, 7, 8, 9, 10, 11],
                            [12, 13, 14, 15]):
                    emit_group(cci, vhs, l_abs, prep_f)
                lastfs = [SLOT[i]["last"] for i in range(NSLOT)]
                assert all(ff[0] == lastfs[0][0] for ff in lastfs)
                gw = [(lastfs[0][0], max(ff[1] for ff in lastfs)), prep_f]
                if cci >= 2:
                    gw.append(gate_done[cci - 2])
                O_f = O_f2[cci % 2]
                rstd_f = rstd2[cci % 2]
                cn_b = cnb2[cci % 2]
                ser.p("vector", lambda e, h, O_f=O_f: e.tensor_tensor(
                    sq_f[:], O_f[:], O_f[:], ALU.mult), waits=gw, inc=("F", 1))
                ser.p("vector", lambda e, h, rstd_f=rstd_f: e.reduce_sum(
                    rstd_f[:], sq_f.rearrange("p (a b) -> p a b", a=NVH), axis=AX.X),
                    waits=[("F", c["F"])], inc=("F", 1))
                ser.p("scalar", lambda e, h, rstd_f=rstd_f: e.activation(
                    rstd_f[:], rstd_f[:], AF.Ln, bias=c_eps6c[:], scale=1.0 / 128.0),
                    waits=[("F", c["F"])], inc=("FS", 1))
                ser.p("scalar", lambda e, h, rstd_f=rstd_f: e.activation(
                    rstd_f[:], rstd_f[:], AF.Exp, scale=-0.5),
                    waits=[("FS", c["FS"])], inc=("FS", 1))
                rf = c["FS"]
                for vh in range(NVH):
                    ser.p("vector", lambda e, h, vh=vh, O_f=O_f, rstd_f=rstd_f,
                          cn_b=cn_b, cb=cb: e.scalar_tensor_tensor(
                              cn_b[:, vh * 128:(vh + 1) * 128],
                              O_f[:, vh * 128:(vh + 1) * 128],
                              rstd_f[:, vh:vh + 1],
                              slz2[cb][:, vh * 128:(vh + 1) * 128],
                              ALU.mult, ALU.mult),
                          waits=([("FS", rf)] if vh == 0 else []), inc=("F", 1))
                gate_done[cci] = ("F", c["F"])
                chunk_last[cci] = ("F", c["F"])
                ser.p("sync", lambda e, h, t0=t0, cn_b=cn_b: e.dma_start(
                    out=cc3_in[t0:t0 + 128, :], in_=cn_b[:, :]),
                    waits=[gate_done[cci]], inc=("S", 16))
            ser.end()

        sec_delta()

        ser.cc(lambda e, h: e.collective_compute(
            "AllGather", ALU.bypass,
            replica_groups=[[0, 1], [2, 3], [4, 5], [6, 7]],
            ins=[cc3_in.opt()], outs=[cc3_out.opt()]))

        for r in range(2):
            ser.d(lambda e, h, r=r: e.dma_start(
                out=s_cnw[:, r * 2048:(r + 1) * 2048],
                in_=cc3_out[bass.DynSlice(h["par"] * TOWN + r * T, TOWN), :]))

        def sec_cnt():
            ser.begin()
            tbs = [WB[:, 0:2048], WB[:, 2048:4096]]
            l_box = {}
            s_box = {}
            for vc in range(32):
                tb = tbs[vc % 2]
                w = [("S", s_box[vc - 2])] if vc >= 2 else []
                ser.p("sync", lambda e, h, vc=vc, tb=tb: e.dma_start_transpose(
                    out=tb[:, :], in_=s_cnw[:, vc * 128:(vc + 1) * 128]),
                    waits=w, inc=("L", 16))
                l_box[vc] = c["L"]
                ser.p("sync", lambda e, h, vc=vc, tb=tb: e.dma_start(
                    out=s_cnf[vc * 128:(vc + 1) * 128, :], in_=tb[:, :]),
                    waits=[("L", l_box[vc])], inc=("S", 16))
                s_box[vc] = c["S"]
            ser.end()

        sec_cnt()
        sec_nproj(32, [(s_wo, 4096)], s_cnf, resid2, resid3, lgw=PC["wo"])

        emit_ln(resid3, s_ln4, TOWN)
        sec_ffn_hidden(s_ln4)
        sec_nproj(HFF // 128, [(qg3v[0:KT], 2816), (qg3v[KT:2 * KT], 2816)],
                  s_hid, resid3, yT)

        holder = {}

        def body_sync(eng):
            holder["par"] = eng.bass.partition_id([SP]) % 2
            ser.emit_for("sync", eng, holder, semh)

        block.sync(body_sync)
        def body_gpsimd(eng):
            holder["par_g"] = eng.bass.partition_id([mybir.EngineType.Pool]) % 2
            ser.emit_for("gpsimd", eng, holder, semh)

        block.gpsimd(body_gpsimd)
        block.vector(lambda eng: ser.emit_for("vector", eng, holder, semh))

        def body_scalar(eng):
            holder["par_s"] = eng.bass.partition_id([mybir.EngineType.Activation]) % 2
            ser.emit_for("scalar", eng, holder, semh)

        block.scalar(body_scalar)
        block.tensor(lambda eng: ser.emit_for("tensor", eng, holder, semh))

    return nc


_NC_CACHE = None


def kernel(L0_attn_a_x, L0_attn_a_W_attn, L0_attn_a_W_proj, L0_attn_b_Wqkv,
           L0_attn_b_Wz, L0_attn_b_Wb, L0_attn_b_Wa, L0_attn_b_conv_w,
           L0_attn_b_dt_bias, L0_attn_b_A_log, L0_attn_b_norm_w, L0_attn_b_Wout,
           L0_ffn_w1, L0_ffn_w2, L0_ffn_w3):
    global _NC_CACHE
    if _NC_CACHE is None:
        _NC_CACHE = build_nc()
    nc = _NC_CACHE

    x = np.asarray(L0_attn_a_x, np.float32)
    Wa_ = np.asarray(L0_attn_a_W_attn, np.float32)
    Wp = np.asarray(L0_attn_a_W_proj, np.float32)
    Wqkv = np.asarray(L0_attn_b_Wqkv, np.float32)
    Wz = np.asarray(L0_attn_b_Wz, np.float32)
    Wb_ = np.asarray(L0_attn_b_Wb, np.float32)
    Waa = np.asarray(L0_attn_b_Wa, np.float32)
    cw = np.asarray(L0_attn_b_conv_w, np.float32)
    dtbv = np.asarray(L0_attn_b_dt_bias, np.float32)
    alogv = np.asarray(L0_attn_b_A_log, np.float32)
    nwv = np.asarray(L0_attn_b_norm_w, np.float32)
    Wo = np.asarray(L0_attn_b_Wout, np.float32)
    w1 = np.asarray(L0_ffn_w1, np.float32)
    w2 = np.asarray(L0_ffn_w2, np.float32)
    w3 = np.asarray(L0_ffn_w3, np.float32)

    ct = lambda a: np.ascontiguousarray(a)
    in_maps = []
    for cc_ in range(8):
        b, s = cc_ // 2, cc_ % 2
        qs, ks, vs = 1024 * s, 2048 + 1024 * s, 4096 + 1024 * s
        in_maps.append({
            "xT": ct(x[b].T),
            "waT": ct(np.concatenate([Wa_[qs:qs + 1024], Wa_[ks:ks + 1024],
                                      Wa_[vs:vs + 1024]], 0).T),
            "wpT": ct(Wp.T),
            "wqkvT": ct(np.concatenate([Wqkv[1024 * s:1024 * s + 1024],
                                        Wqkv[2048 + 1024 * s:2048 + 1024 * s + 1024],
                                        Wqkv[4096 + 2048 * s:4096 + 2048 * s + 2048]], 0).T),
            "wzT": ct(Wz[2048 * s:2048 * s + 2048].T),
            "wbaT": ct(np.concatenate([Wb_[16 * s:16 * s + 16],
                                       Waa[16 * s:16 * s + 16]], 0).T),
            "cwm": ct(np.concatenate([cw[1024 * s:1024 * s + 1024],
                                      cw[2048 + 1024 * s:2048 + 1024 * s + 1024],
                                      cw[4096 + 2048 * s:4096 + 2048 * s + 2048]], 0)),
            "dtb": ct(dtbv[16 * s:16 * s + 16].reshape(16, 1)),
            "alog": ct(alogv[16 * s:16 * s + 16].reshape(16, 1)),
            "nw": ct(nwv.reshape(1, 128)),
            "w1T": ct(w1.T), "w2T": ct(w2.T), "w3T": ct(w3.T),
            "woT": ct(Wo.T),
        })

    res = run_bass_kernel_spmd(nc, in_maps, core_ids=list(range(8)))
    out = np.zeros((B, T, C), np.float32)
    for cc_ in range(8):
        b, s = cc_ // 2, cc_ % 2
        out[b, 2048 * s:2048 * s + 2048, :] = res.results[cc_]["yT"].T
    return out
